# version 28
# baseline (speedup 1.0000x reference)
import math
import numpy as np
import concourse.bass as bass
import concourse.mybir as mybir
from concourse.bass_utils import run_bass_kernel_spmd
from concourse.bass import IndirectOffsetOnAxis

F32 = mybir.dt.float32
BF16 = mybir.dt.bfloat16
U32 = mybir.dt.uint32
I32 = mybir.dt.int32

RACE_SAFE = False
AX = mybir.AluOpType
AF = mybir.ActivationFunctionType
AXL = mybir.AxisListType

D = 1024
NBIN = 1024
TILE = 128
NE = 8
HC = 8


def _host_constants():
    j = np.arange(D, dtype=np.float64)
    d = np.arange(513, dtype=np.float64)
    ang = 2.0 * np.pi * np.outer(j, d) / D
    CS = np.zeros((D, NBIN), dtype=np.float64)
    CS[:, 0:513] = np.cos(ang)
    CS[:, 513:1024] = -np.sin(ang[:, 1:512])
    n = np.arange(D, dtype=np.float64)
    wgt = np.ones(513); wgt[1:512] = 2.0
    MINV = np.zeros((NBIN, D), dtype=np.float64)
    scl = 1.0 / (D * math.sqrt(D))
    MINV[0:513, :] = (wgt[:, None] * np.cos(2 * np.pi * np.outer(d, n) / D)) * scl
    MINV[513:1024, :] = (-2.0 * np.sin(2 * np.pi * np.outer(d[1:512], n) / D)) * scl
    eyeL = np.zeros((TILE, 2 * TILE + 4), dtype=np.float32)
    eyeL[:, 0:TILE] = np.eye(TILE, dtype=np.float32)
    eyeL[:, TILE:2 * TILE] = np.triu(np.ones((TILE, TILE), dtype=np.float32))
    eyeL[:, 2 * TILE + 0] = 0.0
    eyeL[:, 2 * TILE + 1] = 1e-5
    eyeL[:, 2 * TILE + 2] = 1.0
    eyeL[:, 2 * TILE + 3] = -1.0
    return CS.astype(np.float32), MINV.astype(np.float32), eyeL


class W:

    def __init__(self, active, engines, sems):
        self.active = active
        self.eng = engines
        self.sems = sems
        self.cnt = {k: 0 for k in sems}

    _SELF = {"vector": "s_dv", "scalar": "s_ac", "gpsimd": "s_gc"}

    def op(self, ename, meth, *a, waits=(), inc=None, incn=1, **kw):
        own = self._SELF.get(ename)
        if own is not None and self.cnt[own] > 0:
            waits = ((own, self.cnt[own]),) + tuple(waits)
        if self.active == ename:
            e = self.eng[ename]
            for (s, v) in waits:
                if v > 0:
                    e.wait_ge(self.sems[s], v)
            h = getattr(e, meth)(*a, **kw)
            if inc is not None:
                h.then_inc(self.sems[inc], incn)
        if inc is not None:
            self.cnt[inc] += incn

    def wait(self, ename, waits):
        if self.active == ename:
            e = self.eng[ename]
            for (s, v) in waits:
                if v > 0:
                    e.wait_ge(self.sems[s], v)

    def c(self, s):
        return self.cnt[s]


def build_nc(n_cores=8, n_tokens=4096, G=1024):
    NT = n_tokens // TILE
    NG = n_tokens // G
    TPG = G // TILE
    assert n_tokens % G == 0 and G % TILE == 0 and TPG <= 8

    nc = bass.Bass("TRN2", target_bir_lowering=False,
                   detect_race_conditions=RACE_SAFE)

    x_e = nc.dram_tensor("x", [n_tokens, D], F32, kind="ExternalInput").ap()
    wk_e = nc.dram_tensor("Wk", [D, D], F32, kind="ExternalInput").ap()
    wv_e = nc.dram_tensor("Wv", [D, D], F32, kind="ExternalInput").ap()
    wq_e = nc.dram_tensor("Wq", [D, D], F32, kind="ExternalInput").ap()
    bk_e = nc.dram_tensor("bk", [1, D], F32, kind="ExternalInput").ap()
    bv_e = nc.dram_tensor("bv", [1, D], F32, kind="ExternalInput").ap()
    bq_e = nc.dram_tensor("bq", [1, D], F32, kind="ExternalInput").ap()
    gw_e = nc.dram_tensor("gate_w", [NE, D], F32, kind="ExternalInput").ap()
    ew_e = nc.dram_tensor("expert_w", [NE, D, D], F32, kind="ExternalInput").ap()
    cs_e = nc.dram_tensor("CS", [D, NBIN], F32, kind="ExternalInput").ap()
    mi_e = nc.dram_tensor("MINV", [NBIN, D], F32, kind="ExternalInput").ap()
    il_e = nc.dram_tensor("EYEL", [TILE, 2 * TILE + 4], F32, kind="ExternalInput").ap()
    fl_e = nc.dram_tensor("ccflags", [1, 2], F32, kind="ExternalInput").ap()
    NTIL = n_tokens // TILE
    mi_tok = nc.dram_tensor("MOEI", [TILE, NTIL], U32, kind="ExternalInput").ap()
    mf_e = nc.dram_tensor("MOEF", [TILE, NE], F32, kind="ExternalInput").ap()
    tabf_e = nc.dram_tensor("TABF", [NE * max(128, (3 * n_tokens) // 16) + 1, 1],
                            U32, kind="ExternalInput").ap()
    out_e = nc.dram_tensor("out", [n_tokens, D], F32, kind="ExternalOutput").ap()

    wsp_hi = nc.dram_tensor("wsp_hi", [NE, HC, TILE, D], BF16).ap()
    wsp_lo = nc.dram_tensor("wsp_lo", [NE, HC, TILE, D], BF16).ap()
    r1_d = nc.dram_tensor("r1_d", [n_tokens, D], F32).ap()
    kvh_d = nc.dram_tensor("kvh_d", [n_tokens, NBIN], BF16).ap()
    kvl_d = nc.dram_tensor("kvl_d", [n_tokens, NBIN], BF16).ap()
    CAP = max(128, (3 * n_tokens) // 16)
    NSLOT = NE * CAP
    y1x_d = nc.dram_tensor("y1x_d", [n_tokens + 1, D], F32).ap()
    y3x_d = nc.dram_tensor("y3x_d", [n_tokens + 1, D], F32).ap()
    tab_d = nc.dram_tensor("tab_d", [NSLOT + 1, 1], U32).ap()
    awh_d = [nc.dram_tensor(f"awh{i}_d", [HC, TILE, NBIN], BF16).ap() for i in range(4)]
    awl_d = [nc.dram_tensor(f"awl{i}_d", [HC, TILE, NBIN], BF16).ap() for i in range(4)]
    cc_in = nc.dram_tensor("cc_in", [1, NBIN], F32).ap()
    cc_out = nc.dram_tensor("cc_out", [1, NBIN], F32).ap()

    import contextlib
    _es = contextlib.ExitStack()
    with _es:
        pbfA = _es.enter_context(nc.sbuf_tensor("pbfA", [TILE, 8 * NBIN], BF16))
        pbfB = _es.enter_context(nc.sbuf_tensor("pbfB", [TILE, 8 * NBIN], BF16))
        pbfC = _es.enter_context(nc.sbuf_tensor("pbfC", [TILE, 8 * NBIN], BF16))
        pbfD = _es.enter_context(nc.sbuf_tensor("pbfD", [TILE, 8 * NBIN], BF16))
        pbfE = _es.enter_context(nc.sbuf_tensor("pbfE", [TILE, 4 * NBIN], BF16))
        pf32 = _es.enter_context(nc.sbuf_tensor("pf32", [TILE, 8 * D], F32))
        tf0 = _es.enter_context(nc.sbuf_tensor("tf0", [TILE, D], F32))
        tf1 = _es.enter_context(nc.sbuf_tensor("tf1", [TILE, D], F32))
        tf2 = _es.enter_context(nc.sbuf_tensor("tf2", [TILE, D], F32))
        tf3 = _es.enter_context(nc.sbuf_tensor("tf3", [TILE, D], F32))
        tf4 = _es.enter_context(nc.sbuf_tensor("tf4", [TILE, D], F32))
        tf5 = _es.enter_context(nc.sbuf_tensor("tf5", [TILE, D], F32))
        tf6 = _es.enter_context(nc.sbuf_tensor("tf6", [TILE, D], F32))
        tf7 = _es.enter_context(nc.sbuf_tensor("tf7", [TILE, D], F32))
        tf8 = _es.enter_context(nc.sbuf_tensor("tf8", [TILE, D], F32))
        tf9 = _es.enter_context(nc.sbuf_tensor("tf9", [TILE, D], F32))
        ev0 = _es.enter_context(nc.sbuf_tensor("ev0", [TILE, 512], F32))
        ev1 = _es.enter_context(nc.sbuf_tensor("ev1", [TILE, 512], F32))
        tb0 = _es.enter_context(nc.sbuf_tensor("tb0", [TILE, D], BF16))
        tb1 = _es.enter_context(nc.sbuf_tensor("tb1", [TILE, D], BF16))
        tb2 = _es.enter_context(nc.sbuf_tensor("tb2", [TILE, D], BF16))
        tb3 = _es.enter_context(nc.sbuf_tensor("tb3", [TILE, D], BF16))
        tb4 = _es.enter_context(nc.sbuf_tensor("tb4", [TILE, D], BF16))
        tb5 = _es.enter_context(nc.sbuf_tensor("tb5", [TILE, D], BF16))
        tb6 = _es.enter_context(nc.sbuf_tensor("tb6", [TILE, D], BF16))
        tb7 = _es.enter_context(nc.sbuf_tensor("tb7", [TILE, D], BF16))
        cst = _es.enter_context(nc.sbuf_tensor("cst", [TILE, 2 * TILE], BF16))
        csf = _es.enter_context(nc.sbuf_tensor("csf", [TILE, 2 * TILE + 4], F32))
        gwt = _es.enter_context(nc.sbuf_tensor("gwt", [TILE, 2 * NE * HC], BF16))
        onec = _es.enter_context(nc.sbuf_tensor("onec", [TILE, 2], BF16))
        oner = _es.enter_context(nc.sbuf_tensor("oner", [1, TILE], BF16))
        bias6 = _es.enter_context(nc.sbuf_tensor("bias6", [1, 6 * NBIN], BF16))
        brow = _es.enter_context(nc.sbuf_tensor("brow", [1, NBIN], F32))
        brow2 = _es.enter_context(nc.sbuf_tensor("brow2", [1, NBIN], F32))
        carryh = _es.enter_context(nc.sbuf_tensor("carryh", [1, NBIN], BF16))
        carryl = _es.enter_context(nc.sbuf_tensor("carryl", [1, NBIN], BF16))
        flags = _es.enter_context(nc.sbuf_tensor("flags", [1, 2], F32))
        mask1 = _es.enter_context(nc.sbuf_tensor("mask1", [TILE, 32 * NE], F32))
        slots32 = _es.enter_context(nc.sbuf_tensor("slots32", [TILE, 32], I32))
        toksb = _es.enter_context(nc.sbuf_tensor("toksb", [TILE, 32], U32))
        tabsb = _es.enter_context(nc.sbuf_tensor("tabsb", [TILE, 64], I32))
        tabu = _es.enter_context(nc.sbuf_tensor("tabu", [TILE, 128], U32))
        eCb = _es.enter_context(nc.sbuf_tensor("eCb", [TILE, NE], F32))
        pcar = _es.enter_context(nc.sbuf_tensor("pcar", [1, NE], F32))
        post = _es.enter_context(nc.sbuf_tensor("post", [TILE, 2 * NE], F32))
        mask3 = _es.enter_context(nc.sbuf_tensor("mask3", [TILE, 32 * NE], F32))
        ln2sb = _es.enter_context(nc.sbuf_tensor("ln2sb", [TILE, 2 * 32], F32))
        stat = _es.enter_context(nc.sbuf_tensor("stat", [TILE, 24], F32))
        scsb = _es.enter_context(nc.sbuf_tensor("scsb", [NE, 512], F32))
        pk01 = _es.enter_context(nc.psum_tensor("pk01", [TILE, 1024], F32))
        pk23 = _es.enter_context(nc.psum_tensor("pk23", [TILE, 1024], F32))
        ps0 = pk01[:, 0:512]
        ps1 = pk01[:, 512:1024]
        ps2 = pk23[:, 0:512]
        ps3 = pk23[:, 512:1024]
        ps4 = _es.enter_context(nc.psum_tensor("ps4", [TILE, 512], F32))
        ps5 = _es.enter_context(nc.psum_tensor("ps5", [TILE, 512], F32))
        ps6 = _es.enter_context(nc.psum_tensor("ps6", [TILE, 1024], BF16))
        ps7 = _es.enter_context(nc.psum_tensor("ps7", [TILE, 1024], BF16))
        _semnames = ["s_p", "s_x0", "s_x1", "s_k0", "s_k1", "s_w0", "s_w1",
                     "s_aw", "s_g0", "s_g1", "s_sr", "s_sk0", "s_sk1", "s_sa",
                     "s_sy", "s_sp", "s_scc", "s_pe", "s_dv", "s_ac", "s_cc",
                     "s_ig", "s_ig2", "s_gc"]
        _semobjs = {n: _es.enter_context(nc.semaphore(n)) for n in _semnames}
        block = _es.enter_context(nc.Block())
        sems = dict(_semobjs)
        nc.const_aps.register(F32, 0.0, csf[:, 2 * TILE + 0:2 * TILE + 1])
        nc.const_aps.register(F32, 1e-5, csf[:, 2 * TILE + 1:2 * TILE + 2])
        nc.const_aps.register(F32, 1.0, csf[:, 2 * TILE + 2:2 * TILE + 3])
        nc.const_aps.register(F32, -1.0, csf[:, 2 * TILE + 3:2 * TILE + 4])
        PS = [ps0, ps1, ps2, ps3, ps4, ps5, ps6, ps7]

        def emit_all(w: W):
            ident = cst[:, 0:TILE]
            ltri = cst[:, TILE:2 * TILE]
            identf8 = csf[0:NE, 0:NE]
            identf1 = csf[0:1, 0:1]
            brel = {i: ("s_pe", 0) for i in range(8)}

            def mm_group(bank_i, out_ap, ops, extra_first_waits=(), inc_last=True):
                for i, (lt, rr) in enumerate(ops):
                    wl = ()
                    if i == 0:
                        wl = (brel[bank_i],) + tuple(extra_first_waits)
                    w.op("tensor", "matmul", out=out_ap, lhsT=lt, rhs=rr,
                         start=(i == 0), stop=(i == len(ops) - 1), waits=wl,
                         inc="s_pe" if (inc_last and i == len(ops) - 1) else None)
                return w.c("s_pe")

            def evac(bank_i, dst, src, scale=None, bias=0.0, pe_count=None,
                     extra_waits=()):
                wl = []
                if pe_count:
                    wl.append(("s_pe", pe_count))
                wl += list(extra_waits)
                w.op("scalar", "activation", out=dst, in_=src, func=AF.Identity,
                     bias=bias, scale=(1.0 if scale is None else scale),
                     waits=tuple(wl), inc="s_ac")
                for b in (bank_i if isinstance(bank_i, tuple) else (bank_i,)):
                    brel[b] = ("s_ac", w.c("s_ac"))
                return w.c("s_ac")

            def ln_stats(w, src, scale_col, bias_col, eps=1e-5):
                w.op("vector", "bn_stats", out=stat[:, 0:6], in_=src[:, 0:512],
                     inc="s_dv")
                w.op("vector", "bn_stats", out=stat[:, 6:12], in_=src[:, 512:1024],
                     inc="s_dv")
                w.op("vector", "bn_aggr", out=stat[:, 12:14], in_=stat[:, 0:12],
                     inc="s_dv")
                w.op("scalar", "activation", out=stat[:, 16:17], in_=stat[:, 13:14],
                     func=AF.Sqrt, bias=eps, scale=1.0,
                     waits=(("s_dv", w.c("s_dv")),), inc="s_ac")
                w.op("vector", "reciprocal", out=scale_col, in_=stat[:, 16:17],
                     waits=(("s_ac", w.c("s_ac")),), inc="s_dv")
                w.op("vector", "tensor_scalar", out=bias_col, in0=stat[:, 12:13],
                     scalar1=scale_col, scalar2=-1.0, op0=AX.mult, op1=AX.mult,
                     inc="s_dv")

            def split16(dst_hi, dst_lo, src, extra_waits=()):
                w.op("vector", "tensor_copy", out=dst_hi, in_=src,
                     waits=extra_waits, inc="s_dv")
                w.op("vector", "tensor_tensor", out=dst_lo, in0=src, in1=dst_hi,
                     op=AX.subtract, inc="s_dv")
                return w.c("s_dv")

            def transpose8(dst_fn, src_hi, src_lo, tcol, first_waits=()):
                for part, src in enumerate((src_hi, src_lo)):
                    bi = 6 + part
                    pb = PS[bi]
                    for hc in range(HC):
                        wl = ()
                        if hc == 0:
                            wl = (brel[bi],)
                            if part == 0:
                                wl = wl + tuple(first_waits)
                        w.op("tensor", "transpose",
                             out=pb[:, hc * TILE:(hc + 1) * TILE],
                             in_=src[:, hc * TILE:(hc + 1) * TILE],
                             identity=ident, waits=wl,
                             inc="s_pe" if hc == HC - 1 else None)
                    pe_t = w.c("s_pe")
                    w.op("scalar", "activation", out=dst_fn(part, tcol),
                         in_=pb[:, 0:HC * TILE], func=AF.Copy, bias=0.0,
                         scale=1.0, waits=(("s_pe", pe_t),), inc="s_ac")
                    brel[bi] = ("s_ac", w.c("s_ac"))
                return w.c("s_ac")

            w.op("sync", "dma_start", out=csf[:, :], in_=il_e[:, :], inc="s_p", incn=16)
            w.op("sync", "dma_start", out=flags[:, :], in_=fl_e[:, :], inc="s_p", incn=16)
            ld0 = ("s_p", w.c("s_p"))
            w.op("vector", "tensor_copy", out=cst[:, :], in_=csf[:, 0:2 * TILE],
                 waits=(ld0,), inc="s_dv")
            w.op("vector", "memset", onec[:, :], 1.0, inc="s_dv")
            w.op("vector", "memset", oner[:, :], 1.0, inc="s_dv")
            cst_ok = w.c("s_dv")

            p1_dv = [cst_ok, cst_ok]
            p1_ld = {}

            def p1_load(jc):
                par1 = jc % 2
                xsem1 = "s_x0" if par1 == 0 else "s_x1"
                w.op("sync", "dma_start",
                     out=pf32[:, par1 * NBIN:(par1 + 1) * NBIN],
                     in_=cs_e[jc * TILE:(jc + 1) * TILE, :],
                     waits=(("s_dv", p1_dv[par1]),), inc=xsem1, incn=16)
                p1_ld[jc] = (xsem1, w.c(xsem1))

            p1_load(0)
            p1_load(1)
            for jc in range(HC):
                par1 = jc % 2
                w.op("vector", "tensor_copy", out=pbfA[:, jc * NBIN:(jc + 1) * NBIN],
                     in_=pf32[:, par1 * NBIN:(par1 + 1) * NBIN],
                     waits=(p1_ld[jc],), inc="s_dv")
                w.op("vector", "tensor_tensor", out=pbfB[:, jc * NBIN:(jc + 1) * NBIN],
                     in0=pf32[:, par1 * NBIN:(par1 + 1) * NBIN],
                     in1=pbfA[:, jc * NBIN:(jc + 1) * NBIN],
                     op=AX.subtract, inc="s_dv")
                p1_dv[par1] = w.c("s_dv")
                if jc + 2 < HC:
                    p1_load(jc + 2)

            w.op("sync", "dma_start", out=tf3[0:NE, :], in_=gw_e[:, :],
                 waits=(("s_dv", w.c("s_dv")),), inc="s_p", incn=16)
            ldg = ("s_p", w.c("s_p"))
            ops = []
            pe0 = None
            for hc in range(HC):
                wl = (brel[0], ldg, ("s_dv", cst_ok)) if hc == 0 else ()
                w.op("tensor", "transpose", out=ps0[:, hc * NE:(hc + 1) * NE],
                     in_=tf3[0:NE, hc * TILE:(hc + 1) * TILE], identity=identf8,
                     waits=wl, inc="s_pe" if hc == HC - 1 else None)
            pe_g = w.c("s_pe")
            ac_g = evac(0, tf4[:, 0:64], ps0[:, 0:64], pe_count=pe_g)
            w.op("vector", "tensor_copy", out=gwt[:, 0:64], in_=tf4[:, 0:64],
                 waits=(("s_ac", ac_g),), inc="s_dv")
            w.op("vector", "tensor_tensor", out=gwt[:, 64:128], in0=tf4[:, 0:64],
                 in1=gwt[:, 0:64], op=AX.subtract, inc="s_dv")

            p3_w = ((wk_e, bk_e), (wv_e, bv_e), (wq_e, bq_e))
            p3_pe = [0, 0]
            p3_prev_sp = [p1_dv[0], p1_dv[1]]
            p3_dv = {}
            p3_bias_dv = {}
            p3_store_dv = 0

            def p3_load_split(c):
                wi3, hc3 = divmod(c, HC)
                par3 = c % 2
                lbuf = tf0 if par3 == 0 else tf1
                ph3 = tb0 if par3 == 0 else tb6
                pl3 = tb1 if par3 == 0 else tb7
                xsem3 = "s_x0" if par3 == 0 else "s_x1"
                w.op("sync", "dma_start",
                     out=lbuf[:, :].rearrange("p (j c) -> p j c", j=HC),
                     in_=p3_w[wi3][0][:, hc3 * TILE:(hc3 + 1) * TILE]
                         .rearrange("(j p) c -> p j c", p=TILE),
                     waits=(("s_pe", p3_pe[par3]),
                            ("s_dv", p3_prev_sp[par3])),
                     inc=xsem3, incn=16)
                ldw3 = (xsem3, w.c(xsem3))
                p3_dv[c] = split16(ph3[:, :], pl3[:, :], lbuf[:, :],
                                   extra_waits=(ldw3,))
                p3_prev_sp[par3] = p3_dv[c]

            def p3_bias_pro(wi3):
                w.op("sync", "dma_start", out=brow[:, :], in_=p3_w[wi3][1][:, :],
                     waits=(("s_pe", w.c("s_pe")),), inc="s_p", incn=16)
                ldb = ("s_p", w.c("s_p"))
                for jc in range(HC):
                    wl = (brel[4], ldb) if jc == 0 else ()
                    w.op("tensor", "transpose", out=ps4[:, jc:jc + 1],
                         in_=brow[0:1, jc * TILE:(jc + 1) * TILE],
                         identity=identf1, waits=wl,
                         inc="s_pe" if jc == HC - 1 else None)
                pe_b = w.c("s_pe")
                ac_b = evac(4, tf5[:, 0:HC], ps4[:, 0:HC], pe_count=pe_b)
                w.op("vector", "tensor_copy", out=tb4[:, 0:HC], in_=tf5[:, 0:HC],
                     waits=(("s_ac", ac_b),), inc="s_dv")
                w.op("vector", "tensor_tensor", out=tb4[:, HC:2 * HC],
                     in0=tf5[:, 0:HC], in1=tb4[:, 0:HC], op=AX.subtract,
                     inc="s_dv")
                p3_bias_dv[wi3] = w.c("s_dv")

            def p3_bias_epi(wi3):
                for ch in range(2):
                    n0 = ch * 512
                    ops = []
                    for jc in range(HC):
                        bh = tb4[:, jc:jc + 1]
                        bl = tb4[:, HC + jc:HC + jc + 1]
                        rh = pbfA[:, jc * NBIN + n0: jc * NBIN + n0 + 512]
                        rl = pbfB[:, jc * NBIN + n0: jc * NBIN + n0 + 512]
                        ops += [(bh, rh), (bh, rl), (bl, rh)]
                    pe_b2 = mm_group(4 + ch, PS[4 + ch][0:1, :], ops,
                                     extra_first_waits=(("s_dv",
                                                         p3_bias_dv[wi3]),))
                    evac(4 + ch, brow2[0:1, n0:n0 + 512], PS[4 + ch][0:1, :],
                         pe_count=pe_b2)
                ac_b2 = w.c("s_ac")
                w.op("vector", "tensor_copy",
                     out=bias6[:, wi3 * NBIN:(wi3 + 1) * NBIN], in_=brow2[:, :],
                     waits=(("s_ac", ac_b2),), inc="s_dv")
                w.op("vector", "tensor_tensor",
                     out=bias6[:, (3 + wi3) * NBIN:(4 + wi3) * NBIN],
                     in0=brow2[:, :], in1=bias6[:, wi3 * NBIN:(wi3 + 1) * NBIN],
                     op=AX.subtract, inc="s_dv")
                brel[4] = ("s_dv", w.c("s_dv"))
                brel[5] = ("s_dv", w.c("s_dv"))

            def p3_mm(c):
                nonlocal p3_store_dv
                wi3, hc3 = divmod(c, HC)
                par3 = c % 2
                ph3 = tb0 if par3 == 0 else tb6
                pl3 = tb1 if par3 == 0 else tb7
                for ch in range(2):
                    n0 = ch * 512
                    ops = []
                    for jc in range(HC):
                        lh = ph3[:, jc * TILE:(jc + 1) * TILE]
                        ll = pl3[:, jc * TILE:(jc + 1) * TILE]
                        rh = pbfA[:, jc * NBIN + n0: jc * NBIN + n0 + 512]
                        rl = pbfB[:, jc * NBIN + n0: jc * NBIN + n0 + 512]
                        ops += [(lh, rh), (lh, rl), (ll, rh)]
                    pe_w = mm_group(ch, PS[ch][:, :], ops,
                                    extra_first_waits=(("s_dv", p3_dv[c]),))
                    evac(ch, tf3[:, n0:n0 + 512], PS[ch][:, :], pe_count=pe_w,
                         extra_waits=(("s_dv", p3_store_dv),)
                                     if ch == 0 else ())
                p3_pe[par3] = w.c("s_pe")
                ac_w = w.c("s_ac")
                dv_s = split16(tb2[:, :], tb3[:, :], tf3[:, :],
                               extra_waits=(("s_ac", ac_w),
                                            ("s_sp", w.c("s_sp"))))
                p3_store_dv = dv_s
                w.op("gpsimd", "dma_start", out=awh_d[wi3][hc3, :, :],
                     in_=tb2[:, :], waits=(("s_dv", dv_s),),
                     inc="s_sp", incn=16)
                w.op("gpsimd", "dma_start", out=awl_d[wi3][hc3, :, :],
                     in_=tb3[:, :], inc="s_sp", incn=16)

            p3_load_split(0)
            p3_load_split(1)
            for c in range(3 * HC):
                wi, hc = divmod(c, HC)
                if hc == 0:
                    p3_bias_pro(wi)
                p3_mm(c)
                if c + 2 < 3 * HC:
                    p3_load_split(c + 2)
                if hc == HC - 1:
                    p3_bias_epi(wi)

            p4_sp = [p3_prev_sp[0], p3_prev_sp[1]]
            p4_ld = {}

            def p4_load(hc4):
                par4 = hc4 % 2
                xsem4 = "s_x0" if par4 == 0 else "s_x1"
                w.op("sync", "dma_start",
                     out=pf32[:, (2 + par4) * D:(3 + par4) * D],
                     in_=mi_e[hc4 * TILE:(hc4 + 1) * TILE, :],
                     waits=(("s_dv", p4_sp[par4]),), inc=xsem4, incn=16)
                p4_ld[hc4] = (xsem4, w.c(xsem4))

            p4_load(0)
            p4_load(1)
            for hc in range(HC):
                par4 = hc % 2
                dv_m = split16(tb2[:, :], tb3[:, :],
                               pf32[:, (2 + par4) * D:(3 + par4) * D],
                               extra_waits=(p4_ld[hc], ("s_sp", w.c("s_sp"))))
                p4_sp[par4] = dv_m
                if hc + 2 < HC:
                    p4_load(hc + 2)
                w.op("gpsimd", "dma_start", out=awh_d[3][hc, :, :], in_=tb2[:, :],
                     waits=(("s_dv", dv_m),), inc="s_sp", incn=16)
                w.op("gpsimd", "dma_start", out=awl_d[3][hc, :, :], in_=tb3[:, :],
                     inc="s_sp", incn=16)
            p4_quiet = w.c("s_dv")

            p5_slot_pe = [0, 0]
            p5_lde = [("s_k0", 0), ("s_k1", 0)]
            p5_prev_dv = [0, 0]
            p5_st = [0, 0]

            def p5_iter(i5):
                e, fc = i5 // HC, i5 % HC
                sl5 = i5 % 2
                ksem5 = "s_k0" if sl5 == 0 else "s_k1"
                sksem5 = "s_sk0" if sl5 == 0 else "s_sk1"
                stg = tf6 if sl5 == 0 else tf7
                sb_h = tb2 if sl5 == 0 else tb4
                sb_l = tb3 if sl5 == 0 else tb5
                w.op("sync", "dma_start",
                     out=pf32[:, sl5 * D:(sl5 + 1) * D],
                     in_=ew_e[e, fc * TILE:(fc + 1) * TILE, :],
                     waits=(("s_pe", p5_slot_pe[sl5]),
                            ("s_dv", w.c("s_dv") if i5 == 0 else 0)),
                     inc=ksem5, incn=16)
                p5_lde[sl5] = (ksem5, w.c(ksem5))
                lde = p5_lde[sl5]
                for half in range(2):
                    bi = half
                    pb = PS[bi]
                    for q in range(4):
                        hcq = half * 4 + q
                        wl = (brel[bi], lde) if q == 0 else ()
                        w.op("tensor", "transpose",
                             out=pb[:, q * TILE:(q + 1) * TILE],
                             in_=pf32[:, sl5 * D + hcq * TILE:
                                      sl5 * D + (hcq + 1) * TILE],
                             identity=csf[:, 0:TILE], waits=wl,
                             inc="s_pe" if q == 3 else None)
                    pe_e = w.c("s_pe")
                    if half == 1:
                        p5_slot_pe[sl5] = pe_e
                    evac(bi, stg[:, half * 512:half * 512 + 512], pb[:, :],
                         pe_count=pe_e,
                         extra_waits=(("s_dv", p5_prev_dv[sl5]),) if half == 0 else ())
                ac_e = w.c("s_ac")
                dv_e = split16(sb_h[:, :], sb_l[:, :], stg[:, :],
                               extra_waits=(("s_ac", ac_e),
                                            (sksem5, p5_st[sl5]),
                                            ("s_sp", w.c("s_sp") if i5 < 2 else 0)))
                p5_prev_dv[sl5] = dv_e
                w.op("gpsimd", "dma_start",
                     out=wsp_hi[e, :, :, fc * TILE:(fc + 1) * TILE].rearrange("h p c -> p h c"),
                     in_=sb_h[:, :].rearrange("p (h c) -> p h c", h=HC),
                     waits=(("s_dv", dv_e),), inc=sksem5, incn=16)
                w.op("gpsimd", "dma_start",
                     out=wsp_lo[e, :, :, fc * TILE:(fc + 1) * TILE].rearrange("h p c -> p h c"),
                     in_=sb_l[:, :].rearrange("p (h c) -> p h c", h=HC),
                     inc=sksem5, incn=16)
                p5_st[sl5] = w.c(sksem5)

            prep_st = ("s_sp", w.c("s_sp"))
            prep_st2 = (("s_sk0", w.c("s_sk0")), ("s_sk1", w.c("s_sk1")))

            def y2T(part, hc, tc):
                off = part * (8 * TILE)
                return pbfE[:, off + hc * TILE: off + (hc + 1) * TILE]

            def y2T_full(part, tc):
                off = part * (8 * TILE)
                return pbfE[:, off: off + HC * TILE]

            def y3Tsb(part, hc, tc):
                off = (2 + part) * NBIN
                return pbfE[:, off + hc * TILE: off + (hc + 1) * TILE]

            def y3T_full(part, tc):
                off = (2 + part) * NBIN
                return pbfE[:, off: off + HC * TILE]

            def gTf(par):
                def f(part, hc, tc):
                    off = par * 2 * NBIN + part * NBIN
                    return pbfE[:, off + hc * TILE: off + (hc + 1) * TILE]
                return f

            def gT_full(par):
                def f(part, tc):
                    off = par * 2 * NBIN + part * NBIN
                    return pbfE[:, off: off + HC * TILE]
                return f

            NTILES = NT
            NSUB = CAP // TILE

            w.op("sync", "dma_start", out=toksb[:, 0:NTILES], in_=mi_tok[:, :],
                 inc="s_aw", incn=16)
            w.op("sync", "dma_start", out=eCb[:, :], in_=mf_e[:, :],
                 inc="s_aw", incn=16)
            moe_ld = ("s_aw", w.c("s_aw"))

            onesrow_f = csf[0:1, TILE:2 * TILE]
            onescol_f = csf[:, 2 * TILE + 2:2 * TILE + 3]
            ltri_f = csf[:, TILE:2 * TILE]

            def emit_positions(maskbuf, mrel, first_pe_wait):
                w.op("gpsimd", "dma_start", out=tab_d[:, :], in_=tabf_e[:, :],
                     waits=(("s_ig", w.c("s_ig")),), inc="s_ig", incn=16)
                fill_ig = w.c("s_ig")
                w.op("vector", "memset", pcar[:, :], 0.0, inc="s_dv")
                pcar_dv = w.c("s_dv")
                pcar_ac = 0
                colsum_pe = 0
                for t in range(NTILES):
                    mk = maskbuf[:, t * NE:(t + 1) * NE]
                    wl = [brel[4], ("s_dv", mrel[t])]
                    if t == 0:
                        wl += [("s_dv", pcar_dv), ("s_pe", first_pe_wait),
                               moe_ld]
                    else:
                        wl += [("s_ac", pcar_ac)]
                    w.op("tensor", "matmul", out=ps4[:, 0:NE], lhsT=ltri_f,
                         rhs=mk, start=True, stop=False, waits=tuple(wl),
                         skip_group_check=True)
                    w.op("tensor", "matmul", out=ps4[:, 0:NE], lhsT=onesrow_f,
                         rhs=pcar[0:1, :], start=False, stop=True,
                         skip_group_check=True, inc="s_pe")
                    pos_pe = w.c("s_pe")
                    wl2 = (brel[5], ("s_pe", first_pe_wait)) if t == 0 else ()
                    w.op("tensor", "matmul", out=ps5[0:1, 0:NE],
                         lhsT=onescol_f, rhs=mk, start=(t == 0),
                         stop=(t == NTILES - 1), skip_group_check=True,
                         waits=wl2, inc="s_pe")
                    colsum_pe = w.c("s_pe")
                    if t + 1 < NTILES:
                        w.op("scalar", "copy", out=pcar[0:1, :],
                             in_=ps5[0:1, 0:NE], waits=(("s_pe", colsum_pe),),
                             inc="s_ac")
                        pcar_ac = w.c("s_ac")
                    w.op("vector", "tensor_tensor", out=post[:, 0:NE],
                         in0=ps4[:, 0:NE], in1=mk, op=AX.subtract,
                         waits=(("s_pe", pos_pe),), inc="s_dv")
                    brel[4] = ("s_dv", w.c("s_dv"))
                    w.op("vector", "tensor_scalar", out=post[:, NE:2 * NE],
                         in0=post[:, 0:NE], scalar1=float(CAP), scalar2=None,
                         op0=AX.is_ge, op1=AX.bypass, inc="s_dv")
                    w.op("vector", "tensor_tensor", out=post[:, 0:NE],
                         in0=post[:, 0:NE], in1=eCb[:, :], op=AX.add,
                         inc="s_dv")
                    w.op("vector", "scalar_tensor_tensor", out=post[:, 0:NE],
                         in0=post[:, NE:2 * NE], scalar=1e7, op0=AX.mult,
                         op1=AX.add, in1=post[:, 0:NE], inc="s_dv")
                    w.op("vector", "tensor_tensor", out=post[:, 0:NE],
                         in0=post[:, 0:NE], in1=mk, op=AX.mult, inc="s_dv")
                    w.op("vector", "tensor_reduce", out=stat[:, 18:19],
                         in_=post[:, 0:NE], axis=AXL.X, op=AX.add, inc="s_dv")
                    w.op("vector", "tensor_scalar", out=stat[:, 18:19],
                         in0=stat[:, 18:19], scalar1=float(NSLOT),
                         scalar2=None, op0=AX.min, op1=AX.bypass, inc="s_dv")
                    w.op("vector", "tensor_copy", out=slots32[:, t:t + 1],
                         in_=stat[:, 18:19], inc="s_dv")
                    dv_slot = w.c("s_dv")
                    w.op("gpsimd", "indirect_dma_start", out=tab_d[:, :],
                         out_offset=IndirectOffsetOnAxis(
                             ap=slots32[:, t:t + 1], axis=0),
                         in_=toksb[:, t:t + 1], in_offset=None,
                         waits=(("s_ig", w.c("s_ig")), ("s_dv", dv_slot)),
                         inc="s_ig", incn=16)
                return w.c("s_ig")

            wrel2 = [0, 0]
            _bcreg = []

            def get_bcreg():
                if w.active == "gpsimd" and not _bcreg:
                    r = w.eng["gpsimd"].alloc_register("bcreg")
                    w.eng["gpsimd"].reg_mov(r, n_tokens - 1)
                    _bcreg.append(r)
                return _bcreg[0] if _bcreg else None

            def ffn_gather(dst_d, src_d, tab_ig, src_waits, dst_waits):
                nonlocal tr_pe_rel
                G = NSLOT // TILE
                w.op("sync", "dma_start", out=tabu[0:G, 0:TILE],
                     in_=tab_d[0:NSLOT, 0:1].rearrange("(g p) c -> g (p c)",
                                                       p=TILE),
                     waits=(("s_ig", tab_ig), ("s_aw", w.c("s_aw")),
                            ("s_dv", w.c("s_dv"))),
                     inc="s_aw", incn=16)
                tab_ld0 = ("s_aw", w.c("s_aw"))
                w.op("vector", "tensor_copy", out=tf4[0:G, 0:TILE],
                     in_=tabu[0:G, 0:TILE],
                     waits=(tab_ld0, ("s_sy", w.c("s_sy")),
                            ("s_sa", w.c("s_sa")),
                            ("s_sr", w.c("s_sr"))), inc="s_dv")
                dv_tb = w.c("s_dv")
                w.op("tensor", "transpose", out=ps2[:, 0:G],
                     in_=tf4[0:G, 0:TILE], identity=csf[0:G, 0:G],
                     waits=(brel[2], ("s_dv", dv_tb)), inc="s_pe")
                pe_tb = w.c("s_pe")
                ac_tb = evac(2, tf3[:, 0:G], ps2[:, 0:G], pe_count=pe_tb,
                             extra_waits=(("s_sr", w.c("s_sr")),
                                          ("s_sy", w.c("s_sy"))))
                w.op("vector", "tensor_copy", out=tabsb[:, 0:G],
                     in_=tf3[:, 0:G], waits=(("s_ac", ac_tb),), inc="s_dv")
                tab_ld = ("s_dv", w.c("s_dv"))
                w.wait("gpsimd", (tab_ld,))
                grel = [0, 0]
                zrel = [0, 0]
                G_TOT = NE * NSUB
                gcnt = {}
                ldw_e = {}

                def w_load(e):
                    wsem = "s_w0" if e % 2 == 0 else "s_w1"
                    wp_hi = pbfC if e % 2 == 0 else pbfA
                    wp_lo = pbfD if e % 2 == 0 else pbfB
                    w.op("sync", "dma_start",
                         out=wp_hi[:, :].rearrange("p (h c) -> p h c", h=HC),
                         in_=wsp_hi[e, :, :, :].rearrange("h p c -> p h c"),
                         waits=(("s_pe", wrel2[e % 2]), prep_st,
                                ("s_sk0", w.c("s_sk0")),
                                ("s_sk1", w.c("s_sk1"))),
                         inc=wsem, incn=16)
                    w.op("sync", "dma_start",
                         out=wp_lo[:, :].rearrange("p (h c) -> p h c", h=HC),
                         in_=wsp_lo[e, :, :, :].rearrange("h p c -> p h c"),
                         inc=wsem, incn=16)
                    ldw_e[e] = (wsem, w.c(wsem))

                def g_load(g):
                    par = g % 2
                    gsem = "s_g0" if par == 0 else "s_g1"
                    gbuf = tf5 if par == 0 else tf6
                    wl = [("s_dv", grel[par]), tab_ld]
                    if g < 2:
                        wl += list(src_waits)
                    w.op("gpsimd", "indirect_dma_start", out=gbuf[:, :],
                         out_offset=None, in_=src_d[:, :],
                         in_offset=IndirectOffsetOnAxis(
                             ap=tabsb[:, g:g + 1], axis=0),
                         waits=tuple(wl), inc=gsem, incn=16)
                    gcnt[g] = (gsem, w.c(gsem))

                w_load(0)
                g_load(0)
                g_load(1)

                def g_split_transpose(g):
                    nonlocal tr_pe_rel
                    par = g % 2
                    gbuf = tf5 if par == 0 else tf6
                    w.wait("vector", (gcnt[g],))
                    dv_sp = split16(tb0[:, :], tb1[:, :], gbuf[:, :],
                                    extra_waits=(("s_pe", tr_pe_rel),))
                    grel[par] = dv_sp
                    acT[g] = transpose8(gT_full(par), tb0, tb1, 0,
                                       first_waits=(("s_dv", dv_sp),))
                    tr_pe_rel = w.c("s_pe")

                acT = {}
                g_split_transpose(0)
                for g in range(G_TOT):
                    e = g // NSUB
                    par = g % 2
                    zb = tf7 if par == 0 else tf8
                    if g % NSUB == 0 and e + 1 < NE:
                        w_load(e + 1)
                    wp_hi = pbfC if e % 2 == 0 else pbfA
                    wp_lo = pbfD if e % 2 == 0 else pbfB
                    for fj in range(2):
                        bi = (g * 2 + fj) % 2
                        n0 = fj * 512
                        ops = []
                        for hc in range(HC):
                            gh = gTf(par)(0, hc, 0)
                            gl = gTf(par)(1, hc, 0)
                            wh = wp_hi[:, hc * NBIN + n0:
                                       hc * NBIN + n0 + 512]
                            wl_ = wp_lo[:, hc * NBIN + n0:
                                        hc * NBIN + n0 + 512]
                            ops += [(gh, wh), (gh, wl_), (gl, wh)]
                        pe_mm = mm_group(bi, PS[bi][:, :], ops,
                                         extra_first_waits=(ldw_e[e],
                                                            ("s_ac", acT[g])))
                        if g % NSUB == NSUB - 1 and fj == 1:
                            wrel2[e % 2] = pe_mm
                        evac(bi, zb[:, n0:n0 + 512], PS[bi][:, :],
                             scale=0.5, pe_count=pe_mm,
                             extra_waits=(("s_ig" if par == 0 else "s_ig2",
                                           zrel[par]),))
                        if fj == 0:
                            if g + 1 < G_TOT:
                                g_split_transpose(g + 1)
                            if g + 2 < G_TOT:
                                g_load(g + 2)
                    ac_z = w.c("s_ac")
                    zsem = "s_ig" if par == 0 else "s_ig2"
                    wl2 = [("s_ac", ac_z)]
                    if RACE_SAFE:
                        wl2 += [("s_ig", w.c("s_ig")), ("s_ig2", w.c("s_ig2"))]
                    if g < 2:
                        wl2 += list(dst_waits)
                    w.op("gpsimd", "indirect_dma_start", out=dst_d[:, :],
                         out_offset=IndirectOffsetOnAxis(
                             ap=tabsb[:, g:g + 1], axis=0),
                         in_=zb[:, :], in_offset=None,
                         bounds_check=get_bcreg(), oob_is_err=False,
                         compute_op=AX.add,
                         waits=tuple(wl2), inc=zsem, incn=16)
                    zrel[par] = w.c(zsem)
                return (("s_ig", w.c("s_ig")), ("s_ig2", w.c("s_ig2")))

            x_rel = [p4_quiet, p4_quiet]
            tr_pe_rel = 0
            a_prev_split = w.c("s_dv")
            a_y1_st = 0
            a_sp = {}
            a_mrel = {}

            def a_prep(t):
                nonlocal a_prev_split
                xb = tf0 if t % 2 == 0 else tf1
                y1b = tf2 if t % 2 == 0 else tf4
                r1b = tf3 if t % 2 == 0 else tf5
                xsem = "s_x0" if t % 2 == 0 else "s_x1"
                w.op("sync", "dma_start", out=xb[:, :],
                     in_=x_e[t * TILE:(t + 1) * TILE, :],
                     waits=(("s_dv", x_rel[t % 2]),
                            ("s_sy", w.c("s_sy") if t == 0 else 0)),
                     inc=xsem, incn=16)
                ld_x = (xsem, w.c(xsem))
                w.wait("vector", (ld_x,))
                ln_stats(w, xb, stat[:, 14:15], stat[:, 15:16])
                w.op("scalar", "activation", out=y1b[:, :], in_=xb[:, :],
                     func=AF.Identity, bias=stat[:, 15:16],
                     scale=stat[:, 14:15],
                     waits=(("s_dv", a_prev_split), ("s_sr", a_y1_rel[t % 2]),
                            ("s_dv", w.c("s_dv"))), inc="s_ac")
                ac_y = w.c("s_ac")
                w.op("gpsimd", "dma_start",
                     out=y1x_d[t * TILE:(t + 1) * TILE, :], in_=y1b[:, :],
                     waits=(("s_ac", ac_y),), inc="s_sr", incn=16)
                a_y1_rel[t % 2] = w.c("s_sr")
                w.op("vector", "scalar_tensor_tensor", out=r1b[:, :],
                     in0=y1b[:, :], scalar=0.5, in1=xb[:, :], op0=AX.mult,
                     op1=AX.add, waits=(("s_ac", ac_y), ("s_sr", a_r1_rel[t % 2])),
                     inc="s_dv")
                x_rel[t % 2] = w.c("s_dv")
                w.op("gpsimd", "dma_start",
                     out=r1_d[t * TILE:(t + 1) * TILE, :], in_=r1b[:, :],
                     waits=(("s_dv", w.c("s_dv")),), inc="s_sr", incn=16)
                a_r1_rel[t % 2] = w.c("s_sr")
                dv_sp = split16(tb0[:, :], tb1[:, :], y1b[:, :],
                                extra_waits=(("s_pe", tr_pe_rel),))
                a_prev_split = dv_sp
                return dv_sp

            def a_yT_full(part, tc):
                pool = pbfA if part == 0 else pbfB
                ap3 = pool[:, :].rearrange("p (h s) -> p h s", h=HC)
                return ap3[:, :, (tc % 8) * TILE:((tc % 8) + 1) * TILE]

            def a_consume(t):
                nonlocal tr_pe_rel
                transpose8(a_yT_full, tb0, tb1, t,
                           first_waits=(("s_dv", a_sp[t]),))
                tr_pe_rel = w.c("s_pe")
                if t % 4 != 3 and t != NT - 1:
                    return
                nb = t % 4 + 1
                ac_T = w.c("s_ac")
                tq = (t - (nb - 1)) % 8
                ops = []
                for hc in range(HC):
                    gh = gwt[:, hc * NE:(hc + 1) * NE]
                    gl = gwt[:, 64 + hc * NE: 64 + (hc + 1) * NE]
                    base = hc * (8 * TILE) + tq * TILE
                    yh = pbfA[:, base: base + nb * TILE]
                    yl = pbfB[:, base: base + nb * TILE]
                    ops += [(gh, yh), (gh, yl), (gl, yh)]
                pe_s = mm_group(2, ps2[0:NE, 0:nb * TILE], ops,
                                extra_first_waits=(("s_ac", ac_T),))
                ac_s = evac(2, scsb[:, 0:nb * TILE], ps2[0:NE, 0:nb * TILE],
                            pe_count=pe_s)
                for q in range(nb):
                    w.op("tensor", "transpose",
                         out=ps3[:, q * NE:(q + 1) * NE],
                         in_=scsb[0:NE, q * TILE:(q + 1) * TILE],
                         identity=identf8,
                         waits=(brel[3], ("s_ac", ac_s)) if q == 0 else (),
                         inc="s_pe" if q == nb - 1 else None)
                pe_s2 = w.c("s_pe")
                for q in range(nb):
                    tt = t - (nb - 1) + q
                    w.op("vector", "tensor_reduce", out=stat[:, 17:18],
                         in_=ps3[:, q * NE:(q + 1) * NE], axis=AXL.X,
                         op=AX.max,
                         waits=(("s_pe", pe_s2),) if q == 0 else (),
                         inc="s_dv")
                    w.op("vector", "tensor_scalar",
                         out=mask1[:, tt * NE:(tt + 1) * NE],
                         in0=ps3[:, q * NE:(q + 1) * NE],
                         scalar1=stat[:, 17:18], scalar2=None,
                         op0=AX.is_equal, op1=AX.bypass, inc="s_dv")
                    a_mrel[tt] = w.c("s_dv")
                brel[3] = ("s_dv", w.c("s_dv"))

            a_r1_rel = [0, 0]
            a_y1_rel = [0, 0]
            w.op("vector", "memset", tf4[0:1, :], 0.0, inc="s_dv")
            zr_dv = w.c("s_dv")
            w.op("gpsimd", "dma_start", out=y1x_d[n_tokens:n_tokens + 1, :],
                 in_=tf4[0:1, :], waits=(("s_dv", zr_dv),), inc="s_sy", incn=16)
            w.op("gpsimd", "dma_start", out=y3x_d[n_tokens:n_tokens + 1, :],
                 in_=tf4[0:1, :], inc="s_sy", incn=16)

            NP5 = NE * HC
            p5_done = 0
            a_sp[0] = a_prep(0)
            for t in range(NT):
                a_consume(t)
                ntarget = min(NP5, (t + 1) * NP5 // NT)
                while p5_done < ntarget:
                    p5_iter(p5_done)
                    p5_done += 1
                if t + 1 < NT:
                    a_sp[t + 1] = a_prep(t + 1)
            while p5_done < NP5:
                p5_iter(p5_done)
                p5_done += 1

            tab_ig1 = emit_positions(mask1, a_mrel, 0)
            ffn1_done = ffn_gather(r1_d, y1x_d, tab_ig1,
                                   src_waits=(("s_sr", w.c("s_sr")),),
                                   dst_waits=(("s_sr", w.c("s_sr")),))

            pe_all = w.c("s_pe")
            for hc in range(HC):
                wl = (("s_pe", pe_all), prep_st) if hc == 0 else ()
                w.op("sync", "dma_start", out=pbfA[:, hc * NBIN:(hc + 1) * NBIN],
                     in_=awh_d[0][hc, :, :], waits=wl, inc="s_aw", incn=16)
                w.op("sync", "dma_start", out=pbfB[:, hc * NBIN:(hc + 1) * NBIN],
                     in_=awl_d[0][hc, :, :], inc="s_aw", incn=16)
                w.op("sync", "dma_start", out=pbfC[:, hc * NBIN:(hc + 1) * NBIN],
                     in_=awh_d[1][hc, :, :], inc="s_aw", incn=16)
                w.op("sync", "dma_start", out=pbfD[:, hc * NBIN:(hc + 1) * NBIN],
                     in_=awl_d[1][hc, :, :], inc="s_aw", incn=16)
            ld_aw = ("s_aw", w.c("s_aw"))

            r_rel = [w.c("s_dv"), w.c("s_dv")]
            b_gtmp_rel = 0
            kv_st_rel = [0, 0]
            kvb_rel = [0, 0]
            tmp_rel = 0
            prev_split_dv = w.c("s_dv")
            b_ac_T = [0, 0]

            def b_prep(t):
                nonlocal prev_split_dv
                rb = tf0 if t % 2 == 0 else tf1
                xsem = "s_x0" if t % 2 == 0 else "s_x1"
                w.op("sync", "dma_start", out=rb[:, :],
                     in_=r1_d[t * TILE:(t + 1) * TILE, :],
                     waits=(ld_aw, ("s_dv", r_rel[t % 2]),
                            ("s_sr", w.c("s_sr") if t == 0 else 0))
                           + (ffn1_done if t == 0 else ()),
                     inc=xsem, incn=16)
                ld_r = (xsem, w.c(xsem))
                w.wait("vector", (ld_r,))
                sc_col = ln2sb[:, 2 * t:2 * t + 1]
                bi_col = ln2sb[:, 2 * t + 1:2 * t + 2]
                ln_stats(w, rb, sc_col, bi_col)
                w.op("scalar", "activation", out=tf2[:, :], in_=rb[:, :],
                     func=AF.Identity, bias=bi_col, scale=sc_col,
                     waits=(("s_dv", prev_split_dv), ld_r,
                            ("s_dv", w.c("s_dv"))), inc="s_ac")
                ac_y2 = w.c("s_ac")
                w.wait("vector", (("s_ac", ac_y2),))
                dv_sp = split16(tb0[:, :], tb1[:, :], tf2[:, :],
                                extra_waits=(("s_pe", tr_pe_rel),))
                r_rel[t % 2] = dv_sp
                prev_split_dv = dv_sp
                return dv_sp

            def b_kvsum(t):
                kvb_h = tb2 if t % 2 == 0 else tb4
                kvb_l = tb3 if t % 2 == 0 else tb5
                for ch in range(2):
                    n0 = ch * 512
                    pb = PS[4 + ch]
                    wl = (("s_dv", kv_dv[t % 2]),) +                         ((brel[4 + ch],) if t == 0 else ())
                    w.op("tensor", "matmul", out=pb[0:1, :], lhsT=onec[:, 0:1],
                         rhs=kvb_h[:, n0:n0 + 512], start=(t == 0), stop=False,
                         waits=wl if ch == 0 else (brel[4 + ch],) if t == 0 else (),
                         skip_group_check=True)
                    w.op("tensor", "matmul", out=pb[0:1, :], lhsT=onec[:, 0:1],
                         rhs=kvb_l[:, n0:n0 + 512], start=False,
                         stop=(t == NT - 1), skip_group_check=True,
                         inc="s_pe" if ch == 1 else None)
                kvb_rel[t % 2] = w.c("s_pe")

            kv_dv = [0, 0]
            dv_sp0 = b_prep(0)
            sp_dv = {0: dv_sp0}
            for t in range(NT):
                ac_T = transpose8(y2T_full, tb0, tb1, 0,
                                  first_waits=(("s_dv", sp_dv[t]),))
                tr_pe_rel = w.c("s_pe")
                kvpe = [0, 0, 0, 0]
                for wi, (wh_p, wl_p, b0) in enumerate(
                        ((pbfA, pbfB, 0), (pbfC, pbfD, 2))):
                    for ch in range(2):
                        n0 = ch * 512
                        ops = []
                        for hc in range(HC):
                            yh = y2T(0, hc, 0)
                            yl = y2T(1, hc, 0)
                            wh = wh_p[:, hc * NBIN + n0: hc * NBIN + n0 + 512]
                            wl_ = wl_p[:, hc * NBIN + n0: hc * NBIN + n0 + 512]
                            ops += [(yh, wh), (yh, wl_), (yl, wh)]
                        ops += [(oner[:, :], bias6[:, wi * NBIN + n0:
                                                    wi * NBIN + n0 + 512]),
                                (oner[:, :], bias6[:, (3 + wi) * NBIN + n0:
                                                   (3 + wi) * NBIN + n0 + 512])]
                        kvpe[2 * wi + ch] = mm_group(
                            b0 + ch, PS[b0 + ch][:, :], ops,
                            extra_first_waits=(("s_ac", ac_T),))
                if t > 0:
                    b_kvsum(t - 1)
                if t + 1 < NT:
                    sp_dv[t + 1] = b_prep(t + 1)
                evac((0, 1), tf3[:, :], pk01[:, :], pe_count=kvpe[1],
                     extra_waits=(("s_dv", tmp_rel), ("s_gc", w.c("s_gc")),))
                ac_kv = evac((2, 3), tf4[:, :], pk23[:, :], pe_count=kvpe[3])
                kr, ki = tf3[:, 0:513], tf3[:, 513:1024]
                vr, vi = tf4[:, 0:513], tf4[:, 513:1024]
                kr1, vr1 = tf3[:, 1:512], tf4[:, 1:512]
                w.wait("vector", (("s_ac", ac_kv),))
                w.wait("gpsimd", (("s_ac", ac_kv), ("s_dv", b_gtmp_rel)))
                w.op("vector", "tensor_tensor", out=tf5[:, 0:513], in0=kr,
                     in1=vr, op=AX.mult, inc="s_dv")
                w.op("vector", "tensor_tensor", out=tf7[:, 0:511], in0=ki,
                     in1=vi, op=AX.mult, inc="s_dv")
                w.op("vector", "tensor_tensor", out=tf5[:, 1:512],
                     in0=tf5[:, 1:512], in1=tf7[:, 0:511], op=AX.subtract,
                     inc="s_dv")
                w.op("gpsimd", "tensor_tensor", out=tf5[:, 513:1024], in0=kr1,
                     in1=vi, op=AX.mult, inc="s_gc")
                w.op("gpsimd", "tensor_tensor", out=tf7[:, 513:1024], in0=ki,
                     in1=vr1, op=AX.mult, inc="s_gc")
                w.op("gpsimd", "tensor_tensor", out=tf5[:, 513:1024],
                     in0=tf5[:, 513:1024], in1=tf7[:, 513:1024], op=AX.add,
                     inc="s_gc")
                gc_im = w.c("s_gc")
                tmp_rel = w.c("s_dv")
                kvb_h = tb2 if t % 2 == 0 else tb4
                kvb_l = tb3 if t % 2 == 0 else tb5
                sksem = "s_sk0" if t % 2 == 0 else "s_sk1"
                dv_kv = split16(kvb_h[:, :], kvb_l[:, :], tf5[:, :],
                                extra_waits=((sksem, kv_st_rel[t % 2]),
                                             ("s_pe", kvb_rel[t % 2]),
                                             ("s_gc", gc_im)))
                b_gtmp_rel = dv_kv
                kv_dv[t % 2] = dv_kv
                w.op("gpsimd", "dma_start",
                     out=kvh_d[t * TILE:(t + 1) * TILE, :], in_=kvb_h[:, :],
                     waits=(("s_dv", dv_kv),), inc=sksem, incn=16)
                w.op("gpsimd", "dma_start",
                     out=kvl_d[t * TILE:(t + 1) * TILE, :], in_=kvb_l[:, :],
                     inc=sksem, incn=16)
                kv_st_rel[t % 2] = w.c(sksem)
            b_kvsum(NT - 1)
            kv_sum_pe = w.c("s_pe")

            ac_cs = evac(4, brow[0:1, 0:512], ps4[0:1, :], pe_count=kv_sum_pe)
            ac_cs = evac(5, brow[0:1, 512:1024], ps5[0:1, :])
            w.op("vector", "tensor_scalar", out=brow2[:, :], in0=brow[:, :],
                 scalar1=flags[0:1, 0:1], scalar2=None, op0=AX.mult,
                 op1=AX.bypass, waits=(("s_ac", ac_cs),), inc="s_dv")
            dv_cc = w.c("s_dv")
            w.op("gpsimd", "dma_start", out=cc_in[:, :], in_=brow2[:, :],
                 waits=(("s_dv", dv_cc),), inc="s_scc", incn=16)
            w.wait("gpsimd", (("s_scc", w.c("s_scc")),))
            groups = [[2 * i, 2 * i + 1] for i in range(n_cores // 2)]
            w.op("gpsimd", "collective_compute", "AllReduce", AX.add,
                 replica_groups=groups, ins=[cc_in.opt()], outs=[cc_out.opt()],
                 inc="s_cc")
            w.op("gpsimd", "dma_start", out=brow[:, :], in_=cc_out[:, :],
                 waits=(("s_cc", w.c("s_cc")),), inc="s_scc", incn=16)
            st_cc = ("s_scc", w.c("s_scc"))
            w.op("vector", "tensor_scalar", out=brow2[:, :], in0=brow[:, :],
                 scalar1=flags[0:1, 1:2], scalar2=None, op0=AX.mult,
                 op1=AX.bypass, waits=(st_cc,), inc="s_dv")
            cc_ok_dv = w.c("s_dv")

            for hc in range(HC):
                wl = (("s_pe", kv_sum_pe),) if hc == 0 else ()
                w.op("sync", "dma_start", out=pbfA[:, hc * NBIN:(hc + 1) * NBIN],
                     in_=awh_d[2][hc, :, :], waits=wl, inc="s_aw", incn=16)
                w.op("sync", "dma_start", out=pbfB[:, hc * NBIN:(hc + 1) * NBIN],
                     in_=awl_d[2][hc, :, :], inc="s_aw", incn=16)
                w.op("sync", "dma_start", out=pbfC[:, hc * NBIN:(hc + 1) * NBIN],
                     in_=awh_d[3][hc, :, :], inc="s_aw", incn=16)
                w.op("sync", "dma_start", out=pbfD[:, hc * NBIN:(hc + 1) * NBIN],
                     in_=awl_d[3][hc, :, :], inc="s_aw", incn=16)
            ld_aw2 = ("s_aw", w.c("s_aw"))

            c_r_rel = [w.c("s_dv"), w.c("s_dv"), w.c("s_dv")]
            c_kv_rel = [kv_sum_pe, kv_sum_pe]
            c_cum_ac = 0
            c_cum_dv = [0, 0]
            c_ai_st = 0
            c_y3_st = 0
            c_qc_ac = [0, 0]
            c_sp_dv = {}
            c_qpe = {}
            prev_c_split = w.c("s_dv")
            dv_cb = split16(carryh[:, :], carryl[:, :], brow2[:, :],
                            extra_waits=(("s_dv", cc_ok_dv),))
            for ch in range(2):
                n0 = ch * 512
                for i, part in enumerate((carryh, carryl)):
                    w.op("tensor", "matmul", out=PS[4 + ch][0:1, :],
                         lhsT=onec[0:1, 0:1], rhs=part[0:1, n0:n0 + 512],
                         start=(i == 0), stop=False, skip_group_check=True,
                         waits=(brel[4 + ch], ("s_dv", dv_cb)) if i == 0 else (),
                         inc="s_pe" if i == 1 else None)
            carry_pe = w.c("s_pe")
            carry_split_pe = 0

            def c_rb(t):
                m = t % 3
                if m == 0:
                    return tf0[:, :]
                if m == 1:
                    return tf1[:, :]
                return pf32[:, 0:D]

            def c_prep_load(t):
                kvb_h = tb2 if t % 2 == 0 else tb4
                kvb_l = tb3 if t % 2 == 0 else tb5
                xsem = "s_x0" if t % 2 == 0 else "s_x1"
                ksem = "s_k0" if t % 2 == 0 else "s_k1"
                sksem = "s_sk0" if t % 2 == 0 else "s_sk1"
                w.op("sync", "dma_start", out=c_rb(t),
                     in_=r1_d[t * TILE:(t + 1) * TILE, :],
                     waits=(ld_aw2, ("s_dv", c_r_rel[t % 3]),
                            ("s_pe", kv_sum_pe),
                            ("s_ac", w.c("s_ac") if t < 2 else 0)),
                     inc=xsem, incn=16)
                w.op("sync", "dma_start", out=kvb_h[:, :],
                     in_=kvh_d[t * TILE:(t + 1) * TILE, :],
                     waits=(("s_pe", c_kv_rel[t % 2]),
                            (sksem, w.c(sksem) if t < 2 else 0)),
                     inc=ksem, incn=16)
                w.op("sync", "dma_start", out=kvb_l[:, :],
                     in_=kvl_d[t * TILE:(t + 1) * TILE, :],
                     inc=ksem, incn=16)
                return ((xsem, w.c(xsem)), (ksem, w.c(ksem)))

            def c_prep_y2(t, ld_r):
                nonlocal prev_c_split
                sc_col = ln2sb[:, 2 * t:2 * t + 1]
                bi_col = ln2sb[:, 2 * t + 1:2 * t + 2]
                w.op("scalar", "activation", out=tf2[:, :], in_=c_rb(t),
                     func=AF.Identity, bias=bi_col, scale=sc_col,
                     waits=(ld_r, ("s_dv", prev_c_split)), inc="s_ac")
                ac_y2 = w.c("s_ac")
                w.wait("vector", (("s_ac", ac_y2),))
                dv_sp = split16(tb0[:, :], tb1[:, :], tf2[:, :],
                                extra_waits=(("s_pe", tr_pe_rel),))
                prev_c_split = dv_sp
                return dv_sp

            c_acT = {}

            def c_y2T(t):
                nonlocal tr_pe_rel
                c_acT[t] = transpose8(y2T_full, tb0, tb1, 0,
                                      first_waits=(("s_dv", c_sp_dv[t]),))
                tr_pe_rel = w.c("s_pe")

            def c_q_mm(t):
                ac_T = c_acT[t]
                qpe = [0, 0]
                for ch in range(2):
                    n0 = ch * 512
                    ops = []
                    for hc in range(HC):
                        yh = y2T(0, hc, 0)
                        yl = y2T(1, hc, 0)
                        wh = pbfA[:, hc * NBIN + n0: hc * NBIN + n0 + 512]
                        wl_ = pbfB[:, hc * NBIN + n0: hc * NBIN + n0 + 512]
                        ops += [(yh, wh), (yh, wl_), (yl, wh)]
                    ops += [(oner[:, :], bias6[:, 2 * NBIN + n0:
                                                2 * NBIN + n0 + 512]),
                            (oner[:, :], bias6[:, 5 * NBIN + n0:
                                               5 * NBIN + n0 + 512])]
                    qpe[ch] = mm_group(ch, PS[ch][:, :], ops,
                                       extra_first_waits=(("s_ac", ac_T),))
                c_qpe[t] = qpe

            def c_qconj(t):
                qc = tf7 if t % 2 == 0 else tf9
                qpe = c_qpe[t]
                evac((0, 1), qc[:, 0:513], pk01[:, 0:513], pe_count=qpe[1],
                     extra_waits=(("s_dv", c_cum_dv[t % 2]),))
                w.op("scalar", "activation", out=qc[:, 513:1024],
                     in_=pk01[:, 513:1024], func=AF.Identity, bias=0.0,
                     scale=-1.0, inc="s_ac")
                brel[0] = ("s_ac", w.c("s_ac"))
                brel[1] = ("s_ac", w.c("s_ac"))
                c_qc_ac[t % 2] = w.c("s_ac")

            c_cum_ac_arr = [0, 0]
            dv_carry_state = 0
            c_mrel = {}

            def c_carry_prep():
                nonlocal dv_carry_state
                w.op("scalar", "copy", out=brow[0:1, 0:512], in_=ps4[0:1, :],
                     waits=(("s_pe", carry_pe),), inc="s_ac")
                w.op("scalar", "copy", out=brow[0:1, 512:1024], in_=ps5[0:1, :],
                     inc="s_ac")
                ac_carry = w.c("s_ac")
                dv_carry_state = split16(
                    carryh[:, :], carryl[:, :], brow[:, :],
                    extra_waits=(("s_ac", ac_carry),
                                 ("s_pe", carry_split_pe)))

            c_cpe = {}

            def c_cum_mm(t):
                nonlocal carry_pe, carry_split_pe
                kvb_h = tb2 if t % 2 == 0 else tb4
                kvb_l = tb3 if t % 2 == 0 else tb5
                ld_c = c_ld[t][1]
                cpe = [0, 0]
                for ch in range(2):
                    n0 = ch * 512
                    ops = [(oner[:, :], carryh[0:1, n0:n0 + 512]),
                           (oner[:, :], carryl[0:1, n0:n0 + 512]),
                           (ltri, kvb_h[:, n0:n0 + 512]),
                           (ltri, kvb_l[:, n0:n0 + 512])]
                    cpe[ch] = mm_group(2 + ch, PS[2 + ch][:, :], ops,
                                       extra_first_waits=(("s_dv", dv_carry_state),
                                                          ld_c))
                carry_split_pe = w.c("s_pe")
                for ch in range(2):
                    n0 = ch * 512
                    for part in (kvb_h, kvb_l):
                        w.op("tensor", "matmul", out=PS[4 + ch][0:1, :],
                             lhsT=onec[:, 0:1], rhs=part[:, n0:n0 + 512],
                             start=False, stop=(t == NT - 1),
                             skip_group_check=True,
                             inc="s_pe" if (ch == 1 and part is kvb_l) else None)
                carry_pe = w.c("s_pe")
                c_kv_rel[t % 2] = carry_pe
                c_cpe[t] = cpe

            def c_cum_fin(t):
                cumb2 = tf5 if t % 2 == 0 else tf6
                c_cum_ac_arr[t % 2] = evac(
                    (2, 3), cumb2[:, :], pk23[:, :], pe_count=c_cpe[t][1],
                    extra_waits=(("s_dv", c_cum_dv[t % 2]),))
                if t + 1 < NT:
                    c_carry_prep()

            c_carry_prep()
            ld0_pair = c_prep_load(0)
            c_ld = {0: ld0_pair}
            c_sp_dv[0] = c_prep_y2(0, ld0_pair[0])
            c_y2T(0)
            c_q_mm(0)
            c_qconj(0)
            c_ld[1] = c_prep_load(1)
            c_sp_dv[1] = c_prep_y2(1, c_ld[1][0])
            c_y2T(1)
            c_q_mm(1)
            c_cum_mm(0)
            c_cum_fin(0)
            if NT > 1:
                c_cum_mm(1)

            for t in range(NT):
                qc = tf7 if t % 2 == 0 else tf9
                cumb = tf5 if t % 2 == 0 else tf6
                if t + 2 < NT:
                    c_ld[t + 2] = c_prep_load(t + 2)
                ar, ai = cumb[:, 0:513], cumb[:, 513:1024]
                br_, bi_ = qc[:, 0:513], qc[:, 513:1024]
                ar1, br1 = cumb[:, 1:512], qc[:, 1:512]
                w.wait("vector", (("s_ac", c_cum_ac_arr[t % 2]),
                                  ("s_sa", c_ai_st),
                                  ("s_ac", c_qc_ac[t % 2])))
                w.wait("gpsimd", (("s_ac", c_cum_ac_arr[t % 2]),
                                  ("s_ac", c_qc_ac[t % 2]),
                                  ("s_sa", c_ai_st),
                                  ("s_dv", c_cum_dv[(t + 1) % 2])))
                w.op("vector", "tensor_tensor", out=tf8[:, 0:513], in0=ar,
                     in1=br_, op=AX.mult, inc="s_dv")
                w.op("vector", "tensor_tensor", out=tf3[:, 0:511], in0=ai,
                     in1=bi_, op=AX.mult, inc="s_dv")
                w.op("vector", "tensor_tensor", out=tf8[:, 1:512],
                     in0=tf8[:, 1:512], in1=tf3[:, 0:511], op=AX.subtract,
                     inc="s_dv")
                w.op("gpsimd", "tensor_tensor", out=tf8[:, 513:1024], in0=ai,
                     in1=br1, op=AX.mult, inc="s_gc")
                w.op("gpsimd", "tensor_tensor", out=tf3[:, 513:1024], in0=ar1,
                     in1=bi_, op=AX.mult, inc="s_gc")
                w.op("gpsimd", "tensor_tensor", out=tf8[:, 513:1024],
                     in0=tf8[:, 513:1024], in1=tf3[:, 513:1024], op=AX.add,
                     inc="s_gc")
                c_gc_im = w.c("s_gc")
                dv_qq = w.c("s_dv")
                dv_qs = split16(tb0[:, :], tb1[:, :], tf8[:, :],
                                extra_waits=(("s_pe", tr_pe_rel),
                                             ("s_gc", c_gc_im)))
                ac_T2 = transpose8(y3T_full, tb0, tb1, 0,
                                   first_waits=(("s_dv", dv_qs),))
                tr_pe_rel = w.c("s_pe")
                if t + 1 < NT:
                    c_qconj(t + 1)
                if t + 2 < NT:
                    c_sp_dv[t + 2] = c_prep_y2(t + 2, c_ld[t + 2][0])
                w.op("scalar", "activation", out=tf3[:, :], in_=tf8[:, :],
                     func=AF.Square, bias=0.0, scale=1.0,
                     waits=(("s_dv", dv_qq), ("s_gc", c_gc_im)), inc="s_ac")
                ac_sq = w.c("s_ac")
                w.op("vector", "reduce_sum", out=stat[:, 18:19], in_=tf3[:, :],
                     axis=AXL.X, waits=(("s_ac", ac_sq),), inc="s_dv")
                w.op("vector", "tensor_scalar", out=stat[:, 19:20],
                     in0=stat[:, 18:19], scalar1=2.0, scalar2=None,
                     op0=AX.mult, op1=AX.bypass, inc="s_dv")
                w.op("vector", "tensor_tensor", out=stat[:, 19:20],
                     in0=stat[:, 19:20], in1=tf3[:, 0:1], op=AX.subtract,
                     inc="s_dv")
                w.op("vector", "tensor_tensor", out=stat[:, 19:20],
                     in0=stat[:, 19:20], in1=tf3[:, 512:513], op=AX.subtract,
                     inc="s_dv")
                dv_nrm = w.c("s_dv")
                w.op("scalar", "activation", out=stat[:, 20:21],
                     in_=stat[:, 19:20], func=AF.Sqrt, bias=0.0, scale=1.0,
                     waits=(("s_dv", dv_nrm),), inc="s_ac")
                ac_nr = w.c("s_ac")
                w.op("vector", "tensor_scalar", out=stat[:, 21:22],
                     in0=stat[:, 20:21], scalar1=1e-9, scalar2=None,
                     op0=AX.add, op1=AX.bypass, waits=(("s_ac", ac_nr),),
                     inc="s_dv")
                w.op("vector", "reciprocal", out=stat[:, 21:22],
                     in_=stat[:, 21:22], inc="s_dv")
                dv_ri = w.c("s_dv")
                ipe = [0, 0]
                for ch in range(2):
                    n0 = ch * 512
                    ops = []
                    for hc in range(HC):
                        yh = y3Tsb(0, hc, 0)
                        yl = y3Tsb(1, hc, 0)
                        wh = pbfC[:, hc * NBIN + n0: hc * NBIN + n0 + 512]
                        wl_ = pbfD[:, hc * NBIN + n0: hc * NBIN + n0 + 512]
                        ops += [(yh, wh), (yh, wl_), (yl, wh)]
                    ipe[ch] = mm_group(ch, PS[ch][:, :], ops,
                                       extra_first_waits=(("s_ac", ac_T2),
                                                          ("s_ac", max(c_qc_ac)),))
                if t + 2 < NT:
                    c_y2T(t + 2)
                w.op("vector", "scalar_tensor_tensor", out=cumb[:, :],
                     in0=pk01[:, :], scalar=stat[:, 21:22], op0=AX.mult,
                     op1=AX.add, in1=c_rb(t),
                     waits=(("s_pe", ipe[1]), ("s_dv", dv_ri)), inc="s_dv")
                c_r_rel[t % 3] = w.c("s_dv")
                brel[0] = ("s_dv", w.c("s_dv"))
                brel[1] = ("s_dv", w.c("s_dv"))
                if t + 1 < NT:
                    c_cum_fin(t + 1)
                if t + 2 < NT:
                    c_q_mm(t + 2)
                ln_stats(w, cumb, stat[:, 22:23], stat[:, 23:24])
                w.op("scalar", "activation", out=tf4[:, :], in_=cumb[:, :],
                     func=AF.Identity, bias=stat[:, 23:24],
                     scale=stat[:, 22:23],
                     waits=(("s_dv", w.c("s_dv")), ("s_sy", c_y3_st)),
                     inc="s_ac")
                ac_y3 = w.c("s_ac")
                w.op("gpsimd", "dma_start",
                     out=y3x_d[t * TILE:(t + 1) * TILE, :], in_=tf4[:, :],
                     waits=(("s_ac", ac_y3),), inc="s_sy", incn=16)
                c_y3_st = w.c("s_sy")
                w.op("vector", "scalar_tensor_tensor", out=tf8[:, :],
                     in0=tf4[:, :], scalar=0.5, in1=cumb[:, :], op0=AX.mult,
                     op1=AX.add, waits=(("s_ac", ac_y3), ("s_sa", c_ai_st)),
                     inc="s_dv")
                dv_ai = w.c("s_dv")
                c_cum_dv[t % 2] = dv_ai
                w.op("gpsimd", "dma_start",
                     out=out_e[t * TILE:(t + 1) * TILE, :], in_=tf8[:, :],
                     waits=(("s_dv", dv_ai),), inc="s_sa", incn=16)
                c_ai_st = w.c("s_sa")
                w.wait("vector", (("s_ac", ac_y3),))
                dv_y3 = split16(tb0[:, :], tb1[:, :], tf4[:, :],
                                extra_waits=(("s_pe", tr_pe_rel),))
                ac_y3T = transpose8(y3T_full, tb0, tb1, 0,
                                    first_waits=(("s_dv", dv_y3),))
                tr_pe_rel = w.c("s_pe")
                ops = []
                for hc in range(HC):
                    gh = gwt[:, hc * NE:(hc + 1) * NE]
                    gl = gwt[:, 64 + hc * NE: 64 + (hc + 1) * NE]
                    ops += [(gh, y3Tsb(0, hc, 0)), (gh, y3Tsb(1, hc, 0)),
                            (gl, y3Tsb(0, hc, 0))]
                pe_s3 = mm_group(2, ps2[0:NE, 0:TILE], ops,
                                 extra_first_waits=(("s_ac", ac_y3T),))
                ac_s3 = evac(2, scsb[:, 0:TILE], ps2[0:NE, 0:TILE],
                             pe_count=pe_s3)
                w.op("tensor", "transpose", out=ps3[:, 0:NE],
                     in_=scsb[0:NE, 0:TILE], identity=identf8,
                     waits=(brel[3], ("s_ac", ac_s3)), inc="s_pe")
                pe_s4 = w.c("s_pe")
                w.op("vector", "tensor_reduce", out=stat[:, 17:18],
                     in_=ps3[:, 0:NE], axis=AXL.X, op=AX.max,
                     waits=(("s_pe", pe_s4),), inc="s_dv")
                w.op("vector", "tensor_scalar",
                     out=mask3[:, t * NE:(t + 1) * NE], in0=ps3[:, 0:NE],
                     scalar1=stat[:, 17:18], scalar2=None, op0=AX.is_equal,
                     op1=AX.bypass, inc="s_dv")
                c_mrel[t] = w.c("s_dv")
                brel[2] = ("s_dv", w.c("s_dv"))
                brel[3] = ("s_dv", w.c("s_dv"))
                if t + 2 < NT:
                    c_cum_mm(t + 2)

            tab_ig2 = emit_positions(mask3, c_mrel, w.c("s_pe"))
            ffn_gather(out_e, y3x_d, tab_ig2,
                       src_waits=(("s_sy", w.c("s_sy")),),
                       dst_waits=(("s_sa", w.c("s_sa")),))

            w.wait("gpsimd", (("s_sr", w.c("s_sr")), ("s_sk0", w.c("s_sk0")),
                              ("s_sk1", w.c("s_sk1")), ("s_sa", w.c("s_sa")),
                              ("s_sy", w.c("s_sy")), ("s_sp", w.c("s_sp")),
                              ("s_scc", w.c("s_scc")), ("s_ig", w.c("s_ig")),
                              ("s_ig2", w.c("s_ig2")),
                              ("s_g0", w.c("s_g0")), ("s_g1", w.c("s_g1"))))

        def run_pass(name, eng):
            w = W(name, {name: eng}, sems)
            emit_all(w)

        @block.sync
        def _(sync):
            run_pass("sync", sync)

        @block.tensor
        def _(tensor):
            run_pass("tensor", tensor)

        @block.scalar
        def _(scalar):
            run_pass("scalar", scalar)

        @block.vector
        def _(vector):
            run_pass("vector", vector)

        @block.gpsimd
        def _(gpsimd):
            run_pass("gpsimd", gpsimd)

    return nc


def kernel(L0_attn_x, L0_attn_Wk, L0_attn_bk, L0_attn_Wv, L0_attn_bv,
           L0_attn_Wq, L0_attn_bq, L0_ffn_gate_w, L0_ffn_expert_w):
    CS, MINV, EYEL = _host_constants()
    x = np.ascontiguousarray(np.asarray(L0_attn_x, dtype=np.float32))
    B, S, _ = x.shape
    n_cores = 8
    half = S // 2
    nc = build_nc(n_cores=n_cores, n_tokens=half, G=1024)
    CAPh = max(128, (3 * half) // 16)
    ntil = half // 128
    tok = np.empty((128, ntil), np.uint32)
    for c in range(ntil):
        tok[:, c] = c * 128 + np.arange(128)
    eCb_a = np.empty((128, 8), np.float32)
    for e in range(8):
        eCb_a[:, e] = e * CAPh
    tabf = np.full((8 * CAPh + 1, 1), half, np.uint32)
    base = {
        "MOEI": tok, "MOEF": eCb_a, "TABF": tabf,
        "Wk": np.ascontiguousarray(np.asarray(L0_attn_Wk, dtype=np.float32)),
        "Wv": np.ascontiguousarray(np.asarray(L0_attn_Wv, dtype=np.float32)),
        "Wq": np.ascontiguousarray(np.asarray(L0_attn_Wq, dtype=np.float32)),
        "bk": np.ascontiguousarray(np.asarray(L0_attn_bk, dtype=np.float32)).reshape(1, D),
        "bv": np.ascontiguousarray(np.asarray(L0_attn_bv, dtype=np.float32)).reshape(1, D),
        "bq": np.ascontiguousarray(np.asarray(L0_attn_bq, dtype=np.float32)).reshape(1, D),
        "gate_w": np.ascontiguousarray(np.asarray(L0_ffn_gate_w, dtype=np.float32)),
        "expert_w": np.ascontiguousarray(np.asarray(L0_ffn_expert_w, dtype=np.float32)),
        "CS": CS, "MINV": MINV, "EYEL": EYEL,
    }
    in_maps = []
    for c in range(n_cores):
        b, h = c // 2, c % 2
        m = dict(base)
        m["x"] = np.ascontiguousarray(x[b, h * half:(h + 1) * half, :])
        m["ccflags"] = np.array([[1.0 - h, float(h)]], dtype=np.float32)
        in_maps.append(m)
    res = run_bass_kernel_spmd(nc, in_maps, core_ids=list(range(n_cores)))
    out = np.empty((B, S, D), dtype=np.float32)
    for c in range(n_cores):
        b, h = c // 2, c % 2
        out[b, h * half:(h + 1) * half, :] = res.results[c]["out"]
    return out


# version 29 (final)
# speedup vs baseline: 1.0410x; 1.0410x over previous
import math
import numpy as np
import concourse.bass as bass
import concourse.mybir as mybir
from concourse.bass_utils import run_bass_kernel_spmd
from concourse.bass import IndirectOffsetOnAxis

F32 = mybir.dt.float32
BF16 = mybir.dt.bfloat16
U32 = mybir.dt.uint32
I32 = mybir.dt.int32

RACE_SAFE = False
AX = mybir.AluOpType
AF = mybir.ActivationFunctionType
AXL = mybir.AxisListType

D = 1024
NBIN = 1024
TILE = 128
NE = 8
HC = 8


def _host_constants():
    j = np.arange(D, dtype=np.float64)
    d = np.arange(513, dtype=np.float64)
    ang = 2.0 * np.pi * np.outer(j, d) / D
    CS = np.zeros((D, NBIN), dtype=np.float64)
    CS[:, 0:513] = np.cos(ang)
    CS[:, 513:1024] = -np.sin(ang[:, 1:512])
    n = np.arange(D, dtype=np.float64)
    wgt = np.ones(513); wgt[1:512] = 2.0
    MINV = np.zeros((NBIN, D), dtype=np.float64)
    scl = 1.0 / (D * math.sqrt(D))
    MINV[0:513, :] = (wgt[:, None] * np.cos(2 * np.pi * np.outer(d, n) / D)) * scl
    MINV[513:1024, :] = (-2.0 * np.sin(2 * np.pi * np.outer(d[1:512], n) / D)) * scl
    eyeL = np.zeros((TILE, 2 * TILE + 4), dtype=np.float32)
    eyeL[:, 0:TILE] = np.eye(TILE, dtype=np.float32)
    eyeL[:, TILE:2 * TILE] = np.triu(np.ones((TILE, TILE), dtype=np.float32))
    eyeL[:, 2 * TILE + 0] = 0.0
    eyeL[:, 2 * TILE + 1] = 1e-5
    eyeL[:, 2 * TILE + 2] = 1.0
    eyeL[:, 2 * TILE + 3] = -1.0
    return CS.astype(np.float32), MINV.astype(np.float32), eyeL


class W:

    def __init__(self, active, engines, sems):
        self.active = active
        self.eng = engines
        self.sems = sems
        self.cnt = {k: 0 for k in sems}

    _SELF = {"vector": "s_dv", "scalar": "s_ac", "gpsimd": "s_gc"}

    def op(self, ename, meth, *a, waits=(), inc=None, incn=1, **kw):
        own = self._SELF.get(ename)
        if own is not None and self.cnt[own] > 0:
            waits = ((own, self.cnt[own]),) + tuple(waits)
        if self.active == ename:
            e = self.eng[ename]
            for (s, v) in waits:
                if v > 0:
                    e.wait_ge(self.sems[s], v)
            h = getattr(e, meth)(*a, **kw)
            if inc is not None:
                h.then_inc(self.sems[inc], incn)
        if inc is not None:
            self.cnt[inc] += incn

    def wait(self, ename, waits):
        if self.active == ename:
            e = self.eng[ename]
            for (s, v) in waits:
                if v > 0:
                    e.wait_ge(self.sems[s], v)

    def c(self, s):
        return self.cnt[s]


def build_nc(n_cores=8, n_tokens=4096, G=1024):
    NT = n_tokens // TILE
    NG = n_tokens // G
    TPG = G // TILE
    assert n_tokens % G == 0 and G % TILE == 0 and TPG <= 8

    nc = bass.Bass("TRN2", target_bir_lowering=False,
                   detect_race_conditions=RACE_SAFE)

    x_e = nc.dram_tensor("x", [n_tokens, D], F32, kind="ExternalInput").ap()
    wk_e = nc.dram_tensor("Wk", [D, D], F32, kind="ExternalInput").ap()
    wv_e = nc.dram_tensor("Wv", [D, D], F32, kind="ExternalInput").ap()
    wq_e = nc.dram_tensor("Wq", [D, D], F32, kind="ExternalInput").ap()
    bk_e = nc.dram_tensor("bk", [1, D], F32, kind="ExternalInput").ap()
    bv_e = nc.dram_tensor("bv", [1, D], F32, kind="ExternalInput").ap()
    bq_e = nc.dram_tensor("bq", [1, D], F32, kind="ExternalInput").ap()
    gw_e = nc.dram_tensor("gate_w", [NE, D], F32, kind="ExternalInput").ap()
    ew_e = nc.dram_tensor("expert_w", [NE, D, D], F32, kind="ExternalInput").ap()
    cs_e = nc.dram_tensor("CS", [D, NBIN], F32, kind="ExternalInput").ap()
    mi_e = nc.dram_tensor("MINV", [NBIN, D], F32, kind="ExternalInput").ap()
    il_e = nc.dram_tensor("EYEL", [TILE, 2 * TILE + 4], F32, kind="ExternalInput").ap()
    fl_e = nc.dram_tensor("ccflags", [1, 2], F32, kind="ExternalInput").ap()
    NTIL = n_tokens // TILE
    mi_tok = nc.dram_tensor("MOEI", [TILE, NTIL], U32, kind="ExternalInput").ap()
    mf_e = nc.dram_tensor("MOEF", [TILE, NE], F32, kind="ExternalInput").ap()
    tabf_e = nc.dram_tensor("TABF", [NE * max(128, (3 * n_tokens) // 16) + 1, 1],
                            U32, kind="ExternalInput").ap()
    out_e = nc.dram_tensor("out", [n_tokens, D], F32, kind="ExternalOutput").ap()

    wsp_hi = nc.dram_tensor("wsp_hi", [NE, HC, TILE, D], BF16).ap()
    wsp_lo = nc.dram_tensor("wsp_lo", [NE, HC, TILE, D], BF16).ap()
    r1_d = nc.dram_tensor("r1_d", [n_tokens, D], F32).ap()
    kvh_d = nc.dram_tensor("kvh_d", [n_tokens, NBIN], BF16).ap()
    kvl_d = nc.dram_tensor("kvl_d", [n_tokens, NBIN], BF16).ap()
    CAP = max(128, (3 * n_tokens) // 16)
    NSLOT = NE * CAP
    y1x_d = nc.dram_tensor("y1x_d", [n_tokens + 1, D], F32).ap()
    y3x_d = nc.dram_tensor("y3x_d", [n_tokens + 1, D], F32).ap()
    tab_d = nc.dram_tensor("tab_d", [NSLOT + 1, 1], U32).ap()
    awh_d = [nc.dram_tensor(f"awh{i}_d", [HC, TILE, NBIN], BF16).ap() for i in range(4)]
    awl_d = [nc.dram_tensor(f"awl{i}_d", [HC, TILE, NBIN], BF16).ap() for i in range(4)]
    cc_in = nc.dram_tensor("cc_in", [1, NBIN], F32).ap()
    cc_out = nc.dram_tensor("cc_out", [1, NBIN], F32).ap()

    import contextlib
    _es = contextlib.ExitStack()
    with _es:
        pbfA = _es.enter_context(nc.sbuf_tensor("pbfA", [TILE, 8 * NBIN], BF16))
        pbfB = _es.enter_context(nc.sbuf_tensor("pbfB", [TILE, 8 * NBIN], BF16))
        pbfC = _es.enter_context(nc.sbuf_tensor("pbfC", [TILE, 8 * NBIN], BF16))
        pbfD = _es.enter_context(nc.sbuf_tensor("pbfD", [TILE, 8 * NBIN], BF16))
        pbfE = _es.enter_context(nc.sbuf_tensor("pbfE", [TILE, 4 * NBIN], BF16))
        pf32 = _es.enter_context(nc.sbuf_tensor("pf32", [TILE, 8 * D], F32))
        tf0 = _es.enter_context(nc.sbuf_tensor("tf0", [TILE, D], F32))
        tf1 = _es.enter_context(nc.sbuf_tensor("tf1", [TILE, D], F32))
        tf2 = _es.enter_context(nc.sbuf_tensor("tf2", [TILE, D], F32))
        tf3 = _es.enter_context(nc.sbuf_tensor("tf3", [TILE, D], F32))
        tf4 = _es.enter_context(nc.sbuf_tensor("tf4", [TILE, D], F32))
        tf5 = _es.enter_context(nc.sbuf_tensor("tf5", [TILE, D], F32))
        tf6 = _es.enter_context(nc.sbuf_tensor("tf6", [TILE, D], F32))
        tf7 = _es.enter_context(nc.sbuf_tensor("tf7", [TILE, D], F32))
        tf8 = _es.enter_context(nc.sbuf_tensor("tf8", [TILE, D], F32))
        tf9 = _es.enter_context(nc.sbuf_tensor("tf9", [TILE, D], F32))
        ev0 = _es.enter_context(nc.sbuf_tensor("ev0", [TILE, 512], F32))
        ev1 = _es.enter_context(nc.sbuf_tensor("ev1", [TILE, 512], F32))
        tb0 = _es.enter_context(nc.sbuf_tensor("tb0", [TILE, D], BF16))
        tb1 = _es.enter_context(nc.sbuf_tensor("tb1", [TILE, D], BF16))
        tb2 = _es.enter_context(nc.sbuf_tensor("tb2", [TILE, D], BF16))
        tb3 = _es.enter_context(nc.sbuf_tensor("tb3", [TILE, D], BF16))
        tb4 = _es.enter_context(nc.sbuf_tensor("tb4", [TILE, D], BF16))
        tb5 = _es.enter_context(nc.sbuf_tensor("tb5", [TILE, D], BF16))
        tb6 = _es.enter_context(nc.sbuf_tensor("tb6", [TILE, D], BF16))
        tb7 = _es.enter_context(nc.sbuf_tensor("tb7", [TILE, D], BF16))
        cst = _es.enter_context(nc.sbuf_tensor("cst", [TILE, 2 * TILE], BF16))
        csf = _es.enter_context(nc.sbuf_tensor("csf", [TILE, 2 * TILE + 4], F32))
        gwt = _es.enter_context(nc.sbuf_tensor("gwt", [TILE, 2 * NE * HC], BF16))
        onec = _es.enter_context(nc.sbuf_tensor("onec", [TILE, 2], BF16))
        oner = _es.enter_context(nc.sbuf_tensor("oner", [1, TILE], BF16))
        bias6 = _es.enter_context(nc.sbuf_tensor("bias6", [1, 6 * NBIN], BF16))
        brow = _es.enter_context(nc.sbuf_tensor("brow", [1, NBIN], F32))
        brow2 = _es.enter_context(nc.sbuf_tensor("brow2", [1, NBIN], F32))
        carryh = _es.enter_context(nc.sbuf_tensor("carryh", [1, NBIN], BF16))
        carryl = _es.enter_context(nc.sbuf_tensor("carryl", [1, NBIN], BF16))
        flags = _es.enter_context(nc.sbuf_tensor("flags", [1, 2], F32))
        mask1 = _es.enter_context(nc.sbuf_tensor("mask1", [TILE, 32 * NE], F32))
        slots32 = _es.enter_context(nc.sbuf_tensor("slots32", [TILE, 32], I32))
        toksb = _es.enter_context(nc.sbuf_tensor("toksb", [TILE, 32], U32))
        tabsb = _es.enter_context(nc.sbuf_tensor("tabsb", [TILE, 64], I32))
        tabu = _es.enter_context(nc.sbuf_tensor("tabu", [TILE, 128], U32))
        eCb = _es.enter_context(nc.sbuf_tensor("eCb", [TILE, NE], F32))
        pcar = _es.enter_context(nc.sbuf_tensor("pcar", [1, NE], F32))
        post = _es.enter_context(nc.sbuf_tensor("post", [TILE, 2 * NE], F32))
        mask3 = _es.enter_context(nc.sbuf_tensor("mask3", [TILE, 32 * NE], F32))
        ln2sb = _es.enter_context(nc.sbuf_tensor("ln2sb", [TILE, 2 * 32], F32))
        stat = _es.enter_context(nc.sbuf_tensor("stat", [TILE, 24], F32))
        scsb = _es.enter_context(nc.sbuf_tensor("scsb", [NE, 512], F32))
        pk01 = _es.enter_context(nc.psum_tensor("pk01", [TILE, 1024], F32))
        pk23 = _es.enter_context(nc.psum_tensor("pk23", [TILE, 1024], F32))
        ps0 = pk01[:, 0:512]
        ps1 = pk01[:, 512:1024]
        ps2 = pk23[:, 0:512]
        ps3 = pk23[:, 512:1024]
        ps4 = _es.enter_context(nc.psum_tensor("ps4", [TILE, 512], F32))
        ps5 = _es.enter_context(nc.psum_tensor("ps5", [TILE, 512], F32))
        ps6 = _es.enter_context(nc.psum_tensor("ps6", [TILE, 1024], BF16))
        ps7 = _es.enter_context(nc.psum_tensor("ps7", [TILE, 1024], BF16))
        _semnames = ["s_p", "s_x0", "s_x1", "s_k0", "s_k1", "s_w0", "s_w1",
                     "s_aw", "s_g0", "s_g1", "s_sr", "s_sk0", "s_sk1", "s_sa",
                     "s_sy", "s_sp", "s_scc", "s_pe", "s_dv", "s_ac", "s_cc",
                     "s_ig", "s_ig2", "s_gc"]
        _semobjs = {n: _es.enter_context(nc.semaphore(n)) for n in _semnames}
        block = _es.enter_context(nc.Block())
        sems = dict(_semobjs)
        nc.const_aps.register(F32, 0.0, csf[:, 2 * TILE + 0:2 * TILE + 1])
        nc.const_aps.register(F32, 1e-5, csf[:, 2 * TILE + 1:2 * TILE + 2])
        nc.const_aps.register(F32, 1.0, csf[:, 2 * TILE + 2:2 * TILE + 3])
        nc.const_aps.register(F32, -1.0, csf[:, 2 * TILE + 3:2 * TILE + 4])
        PS = [ps0, ps1, ps2, ps3, ps4, ps5, ps6, ps7]

        def emit_all(w: W):
            ident = cst[:, 0:TILE]
            ltri = cst[:, TILE:2 * TILE]
            identf8 = csf[0:NE, 0:NE]
            identf1 = csf[0:1, 0:1]
            brel = {i: ("s_pe", 0) for i in range(8)}

            def mm_group(bank_i, out_ap, ops, extra_first_waits=(), inc_last=True):
                for i, (lt, rr) in enumerate(ops):
                    wl = ()
                    if i == 0:
                        wl = (brel[bank_i],) + tuple(extra_first_waits)
                    w.op("tensor", "matmul", out=out_ap, lhsT=lt, rhs=rr,
                         start=(i == 0), stop=(i == len(ops) - 1), waits=wl,
                         inc="s_pe" if (inc_last and i == len(ops) - 1) else None)
                return w.c("s_pe")

            def evac(bank_i, dst, src, scale=None, bias=0.0, pe_count=None,
                     extra_waits=()):
                wl = []
                if pe_count:
                    wl.append(("s_pe", pe_count))
                wl += list(extra_waits)
                w.op("scalar", "activation", out=dst, in_=src, func=AF.Identity,
                     bias=bias, scale=(1.0 if scale is None else scale),
                     waits=tuple(wl), inc="s_ac")
                for b in (bank_i if isinstance(bank_i, tuple) else (bank_i,)):
                    brel[b] = ("s_ac", w.c("s_ac"))
                return w.c("s_ac")

            def ln_stats(w, src, scale_col, bias_col, eps=1e-5):
                w.op("vector", "bn_stats", out=stat[:, 0:6], in_=src[:, 0:512],
                     inc="s_dv")
                w.op("vector", "bn_stats", out=stat[:, 6:12], in_=src[:, 512:1024],
                     inc="s_dv")
                w.op("vector", "bn_aggr", out=stat[:, 12:14], in_=stat[:, 0:12],
                     inc="s_dv")
                w.op("scalar", "activation", out=stat[:, 16:17], in_=stat[:, 13:14],
                     func=AF.Sqrt, bias=eps, scale=1.0,
                     waits=(("s_dv", w.c("s_dv")),), inc="s_ac")
                w.op("vector", "reciprocal", out=scale_col, in_=stat[:, 16:17],
                     waits=(("s_ac", w.c("s_ac")),), inc="s_dv")
                w.op("vector", "tensor_scalar", out=bias_col, in0=stat[:, 12:13],
                     scalar1=scale_col, scalar2=-1.0, op0=AX.mult, op1=AX.mult,
                     inc="s_dv")

            def split16(dst_hi, dst_lo, src, extra_waits=()):
                w.op("vector", "tensor_copy", out=dst_hi, in_=src,
                     waits=extra_waits, inc="s_dv")
                w.op("vector", "tensor_tensor", out=dst_lo, in0=src, in1=dst_hi,
                     op=AX.subtract, inc="s_dv")
                return w.c("s_dv")

            def transpose8(dst_fn, src_hi, src_lo, tcol, first_waits=()):
                for part, src in enumerate((src_hi, src_lo)):
                    bi = 6 + part
                    pb = PS[bi]
                    for hc in range(HC):
                        wl = ()
                        if hc == 0:
                            wl = (brel[bi],)
                            if part == 0:
                                wl = wl + tuple(first_waits)
                        w.op("tensor", "transpose",
                             out=pb[:, hc * TILE:(hc + 1) * TILE],
                             in_=src[:, hc * TILE:(hc + 1) * TILE],
                             identity=ident, waits=wl,
                             inc="s_pe" if hc == HC - 1 else None)
                    pe_t = w.c("s_pe")
                    w.op("scalar", "activation", out=dst_fn(part, tcol),
                         in_=pb[:, 0:HC * TILE], func=AF.Copy, bias=0.0,
                         scale=1.0, waits=(("s_pe", pe_t),), inc="s_ac")
                    brel[bi] = ("s_ac", w.c("s_ac"))
                return w.c("s_ac")

            w.op("sync", "dma_start", out=csf[:, :], in_=il_e[:, :], inc="s_p", incn=16)
            w.op("sync", "dma_start", out=flags[:, :], in_=fl_e[:, :], inc="s_p", incn=16)
            ld0 = ("s_p", w.c("s_p"))
            w.op("vector", "tensor_copy", out=cst[:, :], in_=csf[:, 0:2 * TILE],
                 waits=(ld0,), inc="s_dv")
            w.op("vector", "memset", onec[:, :], 1.0, inc="s_dv")
            w.op("vector", "memset", oner[:, :], 1.0, inc="s_dv")
            cst_ok = w.c("s_dv")

            p1_dv = [cst_ok, cst_ok]
            p1_ld = {}

            def p1_load(jc):
                par1 = jc % 2
                xsem1 = "s_x0" if par1 == 0 else "s_x1"
                w.op("sync", "dma_start",
                     out=pf32[:, par1 * NBIN:(par1 + 1) * NBIN],
                     in_=cs_e[jc * TILE:(jc + 1) * TILE, :],
                     waits=(("s_dv", p1_dv[par1]),), inc=xsem1, incn=16)
                p1_ld[jc] = (xsem1, w.c(xsem1))

            p1_load(0)
            p1_load(1)
            for jc in range(HC):
                par1 = jc % 2
                w.op("vector", "tensor_copy", out=pbfA[:, jc * NBIN:(jc + 1) * NBIN],
                     in_=pf32[:, par1 * NBIN:(par1 + 1) * NBIN],
                     waits=(p1_ld[jc],), inc="s_dv")
                w.op("vector", "tensor_tensor", out=pbfB[:, jc * NBIN:(jc + 1) * NBIN],
                     in0=pf32[:, par1 * NBIN:(par1 + 1) * NBIN],
                     in1=pbfA[:, jc * NBIN:(jc + 1) * NBIN],
                     op=AX.subtract, inc="s_dv")
                p1_dv[par1] = w.c("s_dv")
                if jc + 2 < HC:
                    p1_load(jc + 2)

            w.op("sync", "dma_start", out=tf3[0:NE, :], in_=gw_e[:, :],
                 waits=(("s_dv", w.c("s_dv")),), inc="s_p", incn=16)
            ldg = ("s_p", w.c("s_p"))
            ops = []
            pe0 = None
            for hc in range(HC):
                wl = (brel[0], ldg, ("s_dv", cst_ok)) if hc == 0 else ()
                w.op("tensor", "transpose", out=ps0[:, hc * NE:(hc + 1) * NE],
                     in_=tf3[0:NE, hc * TILE:(hc + 1) * TILE], identity=identf8,
                     waits=wl, inc="s_pe" if hc == HC - 1 else None)
            pe_g = w.c("s_pe")
            ac_g = evac(0, tf4[:, 0:64], ps0[:, 0:64], pe_count=pe_g)
            w.op("vector", "tensor_copy", out=gwt[:, 0:64], in_=tf4[:, 0:64],
                 waits=(("s_ac", ac_g),), inc="s_dv")
            w.op("vector", "tensor_tensor", out=gwt[:, 64:128], in0=tf4[:, 0:64],
                 in1=gwt[:, 0:64], op=AX.subtract, inc="s_dv")

            p3_w = ((wk_e, bk_e), (wv_e, bv_e), (wq_e, bq_e))
            p3_pe = [0, 0]
            p3_prev_sp = [p1_dv[0], p1_dv[1]]
            p3_dv = {}
            p3_bias_dv = {}
            p3_store_dv = 0

            def p3_load_split(c):
                wi3, hc3 = divmod(c, HC)
                par3 = c % 2
                lbuf = tf0 if par3 == 0 else tf1
                ph3 = tb0 if par3 == 0 else tb6
                pl3 = tb1 if par3 == 0 else tb7
                xsem3 = "s_x0" if par3 == 0 else "s_x1"
                w.op("sync", "dma_start",
                     out=lbuf[:, :].rearrange("p (j c) -> p j c", j=HC),
                     in_=p3_w[wi3][0][:, hc3 * TILE:(hc3 + 1) * TILE]
                         .rearrange("(j p) c -> p j c", p=TILE),
                     waits=(("s_pe", p3_pe[par3]),
                            ("s_dv", p3_prev_sp[par3])),
                     inc=xsem3, incn=16)
                ldw3 = (xsem3, w.c(xsem3))
                p3_dv[c] = split16(ph3[:, :], pl3[:, :], lbuf[:, :],
                                   extra_waits=(ldw3,))
                p3_prev_sp[par3] = p3_dv[c]

            def p3_bias_pro(wi3):
                w.op("sync", "dma_start", out=brow[:, :], in_=p3_w[wi3][1][:, :],
                     waits=(("s_pe", w.c("s_pe")),), inc="s_p", incn=16)
                ldb = ("s_p", w.c("s_p"))
                for jc in range(HC):
                    wl = (brel[4], ldb) if jc == 0 else ()
                    w.op("tensor", "transpose", out=ps4[:, jc:jc + 1],
                         in_=brow[0:1, jc * TILE:(jc + 1) * TILE],
                         identity=identf1, waits=wl,
                         inc="s_pe" if jc == HC - 1 else None)
                pe_b = w.c("s_pe")
                ac_b = evac(4, tf5[:, 0:HC], ps4[:, 0:HC], pe_count=pe_b)
                w.op("vector", "tensor_copy", out=tb4[:, 0:HC], in_=tf5[:, 0:HC],
                     waits=(("s_ac", ac_b),), inc="s_dv")
                w.op("vector", "tensor_tensor", out=tb4[:, HC:2 * HC],
                     in0=tf5[:, 0:HC], in1=tb4[:, 0:HC], op=AX.subtract,
                     inc="s_dv")
                p3_bias_dv[wi3] = w.c("s_dv")

            def p3_bias_epi(wi3):
                for ch in range(2):
                    n0 = ch * 512
                    ops = []
                    for jc in range(HC):
                        bh = tb4[:, jc:jc + 1]
                        bl = tb4[:, HC + jc:HC + jc + 1]
                        rh = pbfA[:, jc * NBIN + n0: jc * NBIN + n0 + 512]
                        rl = pbfB[:, jc * NBIN + n0: jc * NBIN + n0 + 512]
                        ops += [(bh, rh), (bh, rl), (bl, rh)]
                    pe_b2 = mm_group(4 + ch, PS[4 + ch][0:1, :], ops,
                                     extra_first_waits=(("s_dv",
                                                         p3_bias_dv[wi3]),))
                    evac(4 + ch, brow2[0:1, n0:n0 + 512], PS[4 + ch][0:1, :],
                         pe_count=pe_b2)
                ac_b2 = w.c("s_ac")
                w.op("vector", "tensor_copy",
                     out=bias6[:, wi3 * NBIN:(wi3 + 1) * NBIN], in_=brow2[:, :],
                     waits=(("s_ac", ac_b2),), inc="s_dv")
                w.op("vector", "tensor_tensor",
                     out=bias6[:, (3 + wi3) * NBIN:(4 + wi3) * NBIN],
                     in0=brow2[:, :], in1=bias6[:, wi3 * NBIN:(wi3 + 1) * NBIN],
                     op=AX.subtract, inc="s_dv")
                brel[4] = ("s_dv", w.c("s_dv"))
                brel[5] = ("s_dv", w.c("s_dv"))

            def p3_mm(c):
                nonlocal p3_store_dv
                wi3, hc3 = divmod(c, HC)
                par3 = c % 2
                ph3 = tb0 if par3 == 0 else tb6
                pl3 = tb1 if par3 == 0 else tb7
                for ch in range(2):
                    n0 = ch * 512
                    ops = []
                    for jc in range(HC):
                        lh = ph3[:, jc * TILE:(jc + 1) * TILE]
                        ll = pl3[:, jc * TILE:(jc + 1) * TILE]
                        rh = pbfA[:, jc * NBIN + n0: jc * NBIN + n0 + 512]
                        rl = pbfB[:, jc * NBIN + n0: jc * NBIN + n0 + 512]
                        ops += [(lh, rh), (lh, rl), (ll, rh)]
                    pe_w = mm_group(ch, PS[ch][:, :], ops,
                                    extra_first_waits=(("s_dv", p3_dv[c]),))
                    evac(ch, tf3[:, n0:n0 + 512], PS[ch][:, :], pe_count=pe_w,
                         extra_waits=(("s_dv", p3_store_dv),)
                                     if ch == 0 else ())
                p3_pe[par3] = w.c("s_pe")
                ac_w = w.c("s_ac")
                dv_s = split16(tb2[:, :], tb3[:, :], tf3[:, :],
                               extra_waits=(("s_ac", ac_w),
                                            ("s_sp", w.c("s_sp"))))
                p3_store_dv = dv_s
                w.op("gpsimd", "dma_start", out=awh_d[wi3][hc3, :, :],
                     in_=tb2[:, :], waits=(("s_dv", dv_s),),
                     inc="s_sp", incn=16)
                w.op("gpsimd", "dma_start", out=awl_d[wi3][hc3, :, :],
                     in_=tb3[:, :], inc="s_sp", incn=16)

            p3_load_split(0)
            p3_load_split(1)
            for c in range(3 * HC):
                wi, hc = divmod(c, HC)
                if hc == 0:
                    p3_bias_pro(wi)
                p3_mm(c)
                if c + 2 < 3 * HC:
                    p3_load_split(c + 2)
                if hc == HC - 1:
                    p3_bias_epi(wi)

            p4_sp = [p3_prev_sp[0], p3_prev_sp[1]]
            p4_ld = {}

            def p4_load(hc4):
                par4 = hc4 % 2
                xsem4 = "s_x0" if par4 == 0 else "s_x1"
                w.op("sync", "dma_start",
                     out=pf32[:, (2 + par4) * D:(3 + par4) * D],
                     in_=mi_e[hc4 * TILE:(hc4 + 1) * TILE, :],
                     waits=(("s_dv", p4_sp[par4]),), inc=xsem4, incn=16)
                p4_ld[hc4] = (xsem4, w.c(xsem4))

            p4_load(0)
            p4_load(1)
            for hc in range(HC):
                par4 = hc % 2
                dv_m = split16(tb2[:, :], tb3[:, :],
                               pf32[:, (2 + par4) * D:(3 + par4) * D],
                               extra_waits=(p4_ld[hc], ("s_sp", w.c("s_sp"))))
                p4_sp[par4] = dv_m
                if hc + 2 < HC:
                    p4_load(hc + 2)
                w.op("gpsimd", "dma_start", out=awh_d[3][hc, :, :], in_=tb2[:, :],
                     waits=(("s_dv", dv_m),), inc="s_sp", incn=16)
                w.op("gpsimd", "dma_start", out=awl_d[3][hc, :, :], in_=tb3[:, :],
                     inc="s_sp", incn=16)
            p4_quiet = w.c("s_dv")

            p5_slot_pe = [0, 0]
            p5_lde = [("s_k0", 0), ("s_k1", 0)]
            p5_prev_dv = [0, 0]
            p5_st = [0, 0]

            def p5_iter(i5):
                e, fc = i5 // HC, i5 % HC
                sl5 = i5 % 2
                ksem5 = "s_k0" if sl5 == 0 else "s_k1"
                sksem5 = "s_sk0" if sl5 == 0 else "s_sk1"
                stg = tf6 if sl5 == 0 else tf7
                sb_h = tb2 if sl5 == 0 else tb4
                sb_l = tb3 if sl5 == 0 else tb5
                w.op("sync", "dma_start",
                     out=pf32[:, sl5 * D:(sl5 + 1) * D],
                     in_=ew_e[e, fc * TILE:(fc + 1) * TILE, :],
                     waits=(("s_pe", p5_slot_pe[sl5]),
                            ("s_dv", w.c("s_dv") if i5 == 0 else 0)),
                     inc=ksem5, incn=16)
                p5_lde[sl5] = (ksem5, w.c(ksem5))
                lde = p5_lde[sl5]
                for half in range(2):
                    bi = half
                    pb = PS[bi]
                    for q in range(4):
                        hcq = half * 4 + q
                        wl = (brel[bi], lde) if q == 0 else ()
                        w.op("tensor", "transpose",
                             out=pb[:, q * TILE:(q + 1) * TILE],
                             in_=pf32[:, sl5 * D + hcq * TILE:
                                      sl5 * D + (hcq + 1) * TILE],
                             identity=csf[:, 0:TILE], waits=wl,
                             inc="s_pe" if q == 3 else None)
                    pe_e = w.c("s_pe")
                    if half == 1:
                        p5_slot_pe[sl5] = pe_e
                    evac(bi, stg[:, half * 512:half * 512 + 512], pb[:, :],
                         pe_count=pe_e,
                         extra_waits=(("s_dv", p5_prev_dv[sl5]),) if half == 0 else ())
                ac_e = w.c("s_ac")
                dv_e = split16(sb_h[:, :], sb_l[:, :], stg[:, :],
                               extra_waits=(("s_ac", ac_e),
                                            (sksem5, p5_st[sl5]),
                                            ("s_sp", w.c("s_sp") if i5 < 2 else 0)))
                p5_prev_dv[sl5] = dv_e
                w.op("gpsimd", "dma_start",
                     out=wsp_hi[e, :, :, fc * TILE:(fc + 1) * TILE].rearrange("h p c -> p h c"),
                     in_=sb_h[:, :].rearrange("p (h c) -> p h c", h=HC),
                     waits=(("s_dv", dv_e),), inc=sksem5, incn=16)
                w.op("gpsimd", "dma_start",
                     out=wsp_lo[e, :, :, fc * TILE:(fc + 1) * TILE].rearrange("h p c -> p h c"),
                     in_=sb_l[:, :].rearrange("p (h c) -> p h c", h=HC),
                     inc=sksem5, incn=16)
                p5_st[sl5] = w.c(sksem5)

            prep_st = ("s_sp", w.c("s_sp"))
            prep_st2 = (("s_sk0", w.c("s_sk0")), ("s_sk1", w.c("s_sk1")))

            def y2T(part, hc, tc):
                off = part * (8 * TILE)
                return pbfE[:, off + hc * TILE: off + (hc + 1) * TILE]

            def y2T_full(part, tc):
                off = part * (8 * TILE)
                return pbfE[:, off: off + HC * TILE]

            def y3Tsb(part, hc, tc):
                off = (2 + part) * NBIN
                return pbfE[:, off + hc * TILE: off + (hc + 1) * TILE]

            def y3T_full(part, tc):
                off = (2 + part) * NBIN
                return pbfE[:, off: off + HC * TILE]

            def gTf(par):
                def f(part, hc, tc):
                    off = par * 2 * NBIN + part * NBIN
                    return pbfE[:, off + hc * TILE: off + (hc + 1) * TILE]
                return f

            def gT_full(par):
                def f(part, tc):
                    off = par * 2 * NBIN + part * NBIN
                    return pbfE[:, off: off + HC * TILE]
                return f

            NTILES = NT
            NSUB = CAP // TILE

            w.op("sync", "dma_start", out=toksb[:, 0:NTILES], in_=mi_tok[:, :],
                 inc="s_aw", incn=16)
            w.op("sync", "dma_start", out=eCb[:, :], in_=mf_e[:, :],
                 inc="s_aw", incn=16)
            moe_ld = ("s_aw", w.c("s_aw"))

            onesrow_f = csf[0:1, TILE:2 * TILE]
            onescol_f = csf[:, 2 * TILE + 2:2 * TILE + 3]
            ltri_f = csf[:, TILE:2 * TILE]

            def emit_positions(maskbuf, mrel, first_pe_wait):
                w.op("gpsimd", "dma_start", out=tab_d[:, :], in_=tabf_e[:, :],
                     waits=(("s_ig", w.c("s_ig")),), inc="s_ig", incn=16)
                fill_ig = w.c("s_ig")
                w.op("vector", "memset", pcar[:, :], 0.0, inc="s_dv")
                pcar_dv = w.c("s_dv")
                pcar_ac = 0
                colsum_pe = 0
                for t in range(NTILES):
                    mk = maskbuf[:, t * NE:(t + 1) * NE]
                    wl = [brel[4], ("s_dv", mrel[t])]
                    if t == 0:
                        wl += [("s_dv", pcar_dv), ("s_pe", first_pe_wait),
                               moe_ld]
                    else:
                        wl += [("s_ac", pcar_ac)]
                    w.op("tensor", "matmul", out=ps4[:, 0:NE], lhsT=ltri_f,
                         rhs=mk, start=True, stop=False, waits=tuple(wl),
                         skip_group_check=True)
                    w.op("tensor", "matmul", out=ps4[:, 0:NE], lhsT=onesrow_f,
                         rhs=pcar[0:1, :], start=False, stop=True,
                         skip_group_check=True, inc="s_pe")
                    pos_pe = w.c("s_pe")
                    wl2 = (brel[5], ("s_pe", first_pe_wait)) if t == 0 else ()
                    w.op("tensor", "matmul", out=ps5[0:1, 0:NE],
                         lhsT=onescol_f, rhs=mk, start=(t == 0),
                         stop=(t == NTILES - 1), skip_group_check=True,
                         waits=wl2, inc="s_pe")
                    colsum_pe = w.c("s_pe")
                    if t + 1 < NTILES:
                        w.op("scalar", "copy", out=pcar[0:1, :],
                             in_=ps5[0:1, 0:NE], waits=(("s_pe", colsum_pe),),
                             inc="s_ac")
                        pcar_ac = w.c("s_ac")
                    w.op("vector", "tensor_tensor", out=post[:, 0:NE],
                         in0=ps4[:, 0:NE], in1=mk, op=AX.subtract,
                         waits=(("s_pe", pos_pe),), inc="s_dv")
                    brel[4] = ("s_dv", w.c("s_dv"))
                    w.op("vector", "tensor_scalar", out=post[:, NE:2 * NE],
                         in0=post[:, 0:NE], scalar1=float(CAP), scalar2=None,
                         op0=AX.is_ge, op1=AX.bypass, inc="s_dv")
                    w.op("vector", "tensor_tensor", out=post[:, 0:NE],
                         in0=post[:, 0:NE], in1=eCb[:, :], op=AX.add,
                         inc="s_dv")
                    w.op("vector", "scalar_tensor_tensor", out=post[:, 0:NE],
                         in0=post[:, NE:2 * NE], scalar=1e7, op0=AX.mult,
                         op1=AX.add, in1=post[:, 0:NE], inc="s_dv")
                    w.op("vector", "tensor_tensor", out=post[:, 0:NE],
                         in0=post[:, 0:NE], in1=mk, op=AX.mult, inc="s_dv")
                    w.op("vector", "tensor_reduce", out=stat[:, 18:19],
                         in_=post[:, 0:NE], axis=AXL.X, op=AX.add, inc="s_dv")
                    w.op("vector", "tensor_scalar", out=stat[:, 18:19],
                         in0=stat[:, 18:19], scalar1=float(NSLOT),
                         scalar2=None, op0=AX.min, op1=AX.bypass, inc="s_dv")
                    w.op("vector", "tensor_copy", out=slots32[:, t:t + 1],
                         in_=stat[:, 18:19], inc="s_dv")
                    dv_slot = w.c("s_dv")
                    w.op("gpsimd", "indirect_dma_start", out=tab_d[:, :],
                         out_offset=IndirectOffsetOnAxis(
                             ap=slots32[:, t:t + 1], axis=0),
                         in_=toksb[:, t:t + 1], in_offset=None,
                         waits=(("s_ig", w.c("s_ig")), ("s_dv", dv_slot)),
                         inc="s_ig", incn=16)
                return w.c("s_ig")

            wrel2 = [0, 0]
            _bcreg = []

            def get_bcreg():
                if w.active == "gpsimd" and not _bcreg:
                    r = w.eng["gpsimd"].alloc_register("bcreg")
                    w.eng["gpsimd"].reg_mov(r, n_tokens - 1)
                    _bcreg.append(r)
                return _bcreg[0] if _bcreg else None

            def ffn_gather(dst_d, src_d, tab_ig, src_waits, dst_waits):
                nonlocal tr_pe_rel
                G = NSLOT // TILE
                w.op("sync", "dma_start", out=tabu[0:G, 0:TILE],
                     in_=tab_d[0:NSLOT, 0:1].rearrange("(g p) c -> g (p c)",
                                                       p=TILE),
                     waits=(("s_ig", tab_ig), ("s_aw", w.c("s_aw")),
                            ("s_dv", w.c("s_dv"))),
                     inc="s_aw", incn=16)
                tab_ld0 = ("s_aw", w.c("s_aw"))
                w.op("vector", "tensor_copy", out=tf4[0:G, 0:TILE],
                     in_=tabu[0:G, 0:TILE],
                     waits=(tab_ld0, ("s_sy", w.c("s_sy")),
                            ("s_sa", w.c("s_sa")),
                            ("s_sr", w.c("s_sr"))), inc="s_dv")
                dv_tb = w.c("s_dv")
                w.op("tensor", "transpose", out=ps2[:, 0:G],
                     in_=tf4[0:G, 0:TILE], identity=csf[0:G, 0:G],
                     waits=(brel[2], ("s_dv", dv_tb)), inc="s_pe")
                pe_tb = w.c("s_pe")
                ac_tb = evac(2, tf3[:, 0:G], ps2[:, 0:G], pe_count=pe_tb,
                             extra_waits=(("s_sr", w.c("s_sr")),
                                          ("s_sy", w.c("s_sy"))))
                w.op("vector", "tensor_copy", out=tabsb[:, 0:G],
                     in_=tf3[:, 0:G], waits=(("s_ac", ac_tb),), inc="s_dv")
                tab_ld = ("s_dv", w.c("s_dv"))
                w.wait("gpsimd", (tab_ld,))
                grel = [0, 0]
                zrel = [0, 0]
                G_TOT = NE * NSUB
                gcnt = {}
                ldw_e = {}

                def w_load(e):
                    wsem = "s_w0" if e % 2 == 0 else "s_w1"
                    wp_hi = pbfC if e % 2 == 0 else pbfA
                    wp_lo = pbfD if e % 2 == 0 else pbfB
                    w.op("sync", "dma_start",
                         out=wp_hi[:, :].rearrange("p (h c) -> p h c", h=HC),
                         in_=wsp_hi[e, :, :, :].rearrange("h p c -> p h c"),
                         waits=(("s_pe", wrel2[e % 2]), prep_st,
                                ("s_sk0", w.c("s_sk0")),
                                ("s_sk1", w.c("s_sk1"))),
                         inc=wsem, incn=16)
                    w.op("sync", "dma_start",
                         out=wp_lo[:, :].rearrange("p (h c) -> p h c", h=HC),
                         in_=wsp_lo[e, :, :, :].rearrange("h p c -> p h c"),
                         inc=wsem, incn=16)
                    ldw_e[e] = (wsem, w.c(wsem))

                def g_load(g):
                    par = g % 2
                    gsem = "s_g0" if par == 0 else "s_g1"
                    gbuf = tf5 if par == 0 else tf6
                    wl = [("s_dv", grel[par]), tab_ld]
                    if g < 2:
                        wl += list(src_waits)
                    w.op("gpsimd", "indirect_dma_start", out=gbuf[:, :],
                         out_offset=None, in_=src_d[:, :],
                         in_offset=IndirectOffsetOnAxis(
                             ap=tabsb[:, g:g + 1], axis=0),
                         waits=tuple(wl), inc=gsem, incn=16)
                    gcnt[g] = (gsem, w.c(gsem))

                w_load(0)
                g_load(0)
                g_load(1)

                def g_split_transpose(g):
                    nonlocal tr_pe_rel
                    par = g % 2
                    gbuf = tf5 if par == 0 else tf6
                    w.wait("vector", (gcnt[g],))
                    dv_sp = split16(tb0[:, :], tb1[:, :], gbuf[:, :],
                                    extra_waits=(("s_pe", tr_pe_rel),))
                    grel[par] = dv_sp
                    acT[g] = transpose8(gT_full(par), tb0, tb1, 0,
                                       first_waits=(("s_dv", dv_sp),))
                    tr_pe_rel = w.c("s_pe")

                acT = {}
                g_split_transpose(0)
                for g in range(G_TOT):
                    e = g // NSUB
                    par = g % 2
                    zb = tf7 if par == 0 else tf8
                    if g % NSUB == 0 and e + 1 < NE:
                        w_load(e + 1)
                    wp_hi = pbfC if e % 2 == 0 else pbfA
                    wp_lo = pbfD if e % 2 == 0 else pbfB
                    for fj in range(2):
                        bi = (g * 2 + fj) % 2
                        n0 = fj * 512
                        ops = []
                        for hc in range(HC):
                            gh = gTf(par)(0, hc, 0)
                            gl = gTf(par)(1, hc, 0)
                            wh = wp_hi[:, hc * NBIN + n0:
                                       hc * NBIN + n0 + 512]
                            wl_ = wp_lo[:, hc * NBIN + n0:
                                        hc * NBIN + n0 + 512]
                            ops += [(gh, wh), (gh, wl_), (gl, wh)]
                        pe_mm = mm_group(bi, PS[bi][:, :], ops,
                                         extra_first_waits=(ldw_e[e],
                                                            ("s_ac", acT[g])))
                        if g % NSUB == NSUB - 1 and fj == 1:
                            wrel2[e % 2] = pe_mm
                        evac(bi, zb[:, n0:n0 + 512], PS[bi][:, :],
                             scale=0.5, pe_count=pe_mm,
                             extra_waits=(("s_ig" if par == 0 else "s_ig2",
                                           zrel[par]),))
                        if fj == 0:
                            if g + 1 < G_TOT:
                                g_split_transpose(g + 1)
                            if g + 2 < G_TOT:
                                g_load(g + 2)
                    ac_z = w.c("s_ac")
                    zsem = "s_ig" if par == 0 else "s_ig2"
                    wl2 = [("s_ac", ac_z)]
                    if RACE_SAFE:
                        wl2 += [("s_ig", w.c("s_ig")), ("s_ig2", w.c("s_ig2"))]
                    if g < 2:
                        wl2 += list(dst_waits)
                    w.op("gpsimd", "indirect_dma_start", out=dst_d[:, :],
                         out_offset=IndirectOffsetOnAxis(
                             ap=tabsb[:, g:g + 1], axis=0),
                         in_=zb[:, :], in_offset=None,
                         bounds_check=get_bcreg(), oob_is_err=False,
                         compute_op=AX.add,
                         waits=tuple(wl2), inc=zsem, incn=16)
                    zrel[par] = w.c(zsem)
                return (("s_ig", w.c("s_ig")), ("s_ig2", w.c("s_ig2")))

            x_rel = [p4_quiet, p4_quiet]
            tr_pe_rel = 0
            a_prev_split = w.c("s_dv")
            a_y1_st = 0
            a_sp = {}
            a_mrel = {}

            def a_prep(t):
                nonlocal a_prev_split
                xb = tf0 if t % 2 == 0 else tf1
                y1b = tf2 if t % 2 == 0 else tf4
                r1b = tf3 if t % 2 == 0 else tf5
                xsem = "s_x0" if t % 2 == 0 else "s_x1"
                w.op("sync", "dma_start", out=xb[:, :],
                     in_=x_e[t * TILE:(t + 1) * TILE, :],
                     waits=(("s_dv", x_rel[t % 2]),
                            ("s_sy", w.c("s_sy") if t == 0 else 0)),
                     inc=xsem, incn=16)
                ld_x = (xsem, w.c(xsem))
                w.wait("vector", (ld_x,))
                ln_stats(w, xb, stat[:, 14:15], stat[:, 15:16])
                w.op("scalar", "activation", out=y1b[:, :], in_=xb[:, :],
                     func=AF.Identity, bias=stat[:, 15:16],
                     scale=stat[:, 14:15],
                     waits=(("s_dv", a_prev_split), ("s_sr", a_y1_rel[t % 2]),
                            ("s_dv", w.c("s_dv"))), inc="s_ac")
                ac_y = w.c("s_ac")
                dv_sp = split16(tb0[:, :], tb1[:, :], y1b[:, :],
                                extra_waits=(("s_ac", ac_y),
                                             ("s_pe", tr_pe_rel)))
                a_prev_split = dv_sp
                w.op("gpsimd", "dma_start",
                     out=y1x_d[t * TILE:(t + 1) * TILE, :], in_=y1b[:, :],
                     waits=(("s_ac", ac_y),), inc="s_sr", incn=16)
                a_y1_rel[t % 2] = w.c("s_sr")
                w.op("vector", "scalar_tensor_tensor", out=r1b[:, :],
                     in0=y1b[:, :], scalar=0.5, in1=xb[:, :], op0=AX.mult,
                     op1=AX.add, waits=(("s_ac", ac_y), ("s_sr", a_r1_rel[t % 2])),
                     inc="s_dv")
                x_rel[t % 2] = w.c("s_dv")
                w.op("gpsimd", "dma_start",
                     out=r1_d[t * TILE:(t + 1) * TILE, :], in_=r1b[:, :],
                     waits=(("s_dv", w.c("s_dv")),), inc="s_sr", incn=16)
                a_r1_rel[t % 2] = w.c("s_sr")
                return dv_sp

            def a_yT_full(part, tc):
                pool = pbfA if part == 0 else pbfB
                ap3 = pool[:, :].rearrange("p (h s) -> p h s", h=HC)
                return ap3[:, :, (tc % 8) * TILE:((tc % 8) + 1) * TILE]

            def a_consume(t):
                nonlocal tr_pe_rel
                transpose8(a_yT_full, tb0, tb1, t,
                           first_waits=(("s_dv", a_sp[t]),))
                tr_pe_rel = w.c("s_pe")
                if t % 4 != 3 and t != NT - 1:
                    return
                nb = t % 4 + 1
                ac_T = w.c("s_ac")
                tq = (t - (nb - 1)) % 8
                ops = []
                for hc in range(HC):
                    gh = gwt[:, hc * NE:(hc + 1) * NE]
                    gl = gwt[:, 64 + hc * NE: 64 + (hc + 1) * NE]
                    base = hc * (8 * TILE) + tq * TILE
                    yh = pbfA[:, base: base + nb * TILE]
                    yl = pbfB[:, base: base + nb * TILE]
                    ops += [(gh, yh), (gh, yl), (gl, yh)]
                pe_s = mm_group(2, ps2[0:NE, 0:nb * TILE], ops,
                                extra_first_waits=(("s_ac", ac_T),))
                ac_s = evac(2, scsb[:, 0:nb * TILE], ps2[0:NE, 0:nb * TILE],
                            pe_count=pe_s)
                for q in range(nb):
                    w.op("tensor", "transpose",
                         out=ps3[:, q * NE:(q + 1) * NE],
                         in_=scsb[0:NE, q * TILE:(q + 1) * TILE],
                         identity=identf8,
                         waits=(brel[3], ("s_ac", ac_s)) if q == 0 else (),
                         inc="s_pe" if q == nb - 1 else None)
                pe_s2 = w.c("s_pe")
                for q in range(nb):
                    tt = t - (nb - 1) + q
                    w.op("vector", "tensor_reduce", out=stat[:, 17:18],
                         in_=ps3[:, q * NE:(q + 1) * NE], axis=AXL.X,
                         op=AX.max,
                         waits=(("s_pe", pe_s2),) if q == 0 else (),
                         inc="s_dv")
                    w.op("vector", "tensor_scalar",
                         out=mask1[:, tt * NE:(tt + 1) * NE],
                         in0=ps3[:, q * NE:(q + 1) * NE],
                         scalar1=stat[:, 17:18], scalar2=None,
                         op0=AX.is_equal, op1=AX.bypass, inc="s_dv")
                    a_mrel[tt] = w.c("s_dv")
                brel[3] = ("s_dv", w.c("s_dv"))

            a_r1_rel = [0, 0]
            a_y1_rel = [0, 0]
            w.op("vector", "memset", tf4[0:1, :], 0.0, inc="s_dv")
            zr_dv = w.c("s_dv")
            w.op("gpsimd", "dma_start", out=y1x_d[n_tokens:n_tokens + 1, :],
                 in_=tf4[0:1, :], waits=(("s_dv", zr_dv),), inc="s_sy", incn=16)
            w.op("gpsimd", "dma_start", out=y3x_d[n_tokens:n_tokens + 1, :],
                 in_=tf4[0:1, :], inc="s_sy", incn=16)

            NP5 = NE * HC
            p5_done = 0
            a_sp[0] = a_prep(0)
            for t in range(NT):
                a_consume(t)
                ntarget = min(NP5, (t + 1) * NP5 // NT)
                while p5_done < ntarget:
                    p5_iter(p5_done)
                    p5_done += 1
                if t + 1 < NT:
                    a_sp[t + 1] = a_prep(t + 1)
            while p5_done < NP5:
                p5_iter(p5_done)
                p5_done += 1

            tab_ig1 = emit_positions(mask1, a_mrel, 0)
            ffn1_done = ffn_gather(r1_d, y1x_d, tab_ig1,
                                   src_waits=(("s_sr", w.c("s_sr")),),
                                   dst_waits=(("s_sr", w.c("s_sr")),))

            pe_all = w.c("s_pe")
            for hc in range(HC):
                wl = (("s_pe", pe_all), prep_st) if hc == 0 else ()
                w.op("sync", "dma_start", out=pbfA[:, hc * NBIN:(hc + 1) * NBIN],
                     in_=awh_d[0][hc, :, :], waits=wl, inc="s_aw", incn=16)
                w.op("sync", "dma_start", out=pbfB[:, hc * NBIN:(hc + 1) * NBIN],
                     in_=awl_d[0][hc, :, :], inc="s_aw", incn=16)
                w.op("sync", "dma_start", out=pbfC[:, hc * NBIN:(hc + 1) * NBIN],
                     in_=awh_d[1][hc, :, :], inc="s_aw", incn=16)
                w.op("sync", "dma_start", out=pbfD[:, hc * NBIN:(hc + 1) * NBIN],
                     in_=awl_d[1][hc, :, :], inc="s_aw", incn=16)
            ld_aw = ("s_aw", w.c("s_aw"))

            r_rel = [w.c("s_dv"), w.c("s_dv")]
            b_gtmp_rel = 0
            kv_st_rel = [0, 0]
            kvb_rel = [0, 0]
            tmp_rel = 0
            prev_split_dv = w.c("s_dv")
            b_ac_T = [0, 0]

            def b_prep(t):
                nonlocal prev_split_dv
                rb = tf0 if t % 2 == 0 else tf1
                xsem = "s_x0" if t % 2 == 0 else "s_x1"
                w.op("sync", "dma_start", out=rb[:, :],
                     in_=r1_d[t * TILE:(t + 1) * TILE, :],
                     waits=(ld_aw, ("s_dv", r_rel[t % 2]),
                            ("s_sr", w.c("s_sr") if t == 0 else 0))
                           + (ffn1_done if t == 0 else ()),
                     inc=xsem, incn=16)
                ld_r = (xsem, w.c(xsem))
                w.wait("vector", (ld_r,))
                sc_col = ln2sb[:, 2 * t:2 * t + 1]
                bi_col = ln2sb[:, 2 * t + 1:2 * t + 2]
                ln_stats(w, rb, sc_col, bi_col)
                w.op("scalar", "activation", out=tf2[:, :], in_=rb[:, :],
                     func=AF.Identity, bias=bi_col, scale=sc_col,
                     waits=(("s_dv", prev_split_dv), ld_r,
                            ("s_dv", w.c("s_dv"))), inc="s_ac")
                ac_y2 = w.c("s_ac")
                w.wait("vector", (("s_ac", ac_y2),))
                dv_sp = split16(tb0[:, :], tb1[:, :], tf2[:, :],
                                extra_waits=(("s_pe", tr_pe_rel),))
                r_rel[t % 2] = dv_sp
                prev_split_dv = dv_sp
                return dv_sp

            def b_kvsum(t):
                kvb_h = tb2 if t % 2 == 0 else tb4
                kvb_l = tb3 if t % 2 == 0 else tb5
                for ch in range(2):
                    n0 = ch * 512
                    pb = PS[4 + ch]
                    wl = (("s_dv", kv_dv[t % 2]),) +                         ((brel[4 + ch],) if t == 0 else ())
                    w.op("tensor", "matmul", out=pb[0:1, :], lhsT=onec[:, 0:1],
                         rhs=kvb_h[:, n0:n0 + 512], start=(t == 0), stop=False,
                         waits=wl if ch == 0 else (brel[4 + ch],) if t == 0 else (),
                         skip_group_check=True)
                    w.op("tensor", "matmul", out=pb[0:1, :], lhsT=onec[:, 0:1],
                         rhs=kvb_l[:, n0:n0 + 512], start=False,
                         stop=(t == NT - 1), skip_group_check=True,
                         inc="s_pe" if ch == 1 else None)
                kvb_rel[t % 2] = w.c("s_pe")

            kv_dv = [0, 0]
            dv_sp0 = b_prep(0)
            sp_dv = {0: dv_sp0}
            for t in range(NT):
                ac_T = transpose8(y2T_full, tb0, tb1, 0,
                                  first_waits=(("s_dv", sp_dv[t]),))
                tr_pe_rel = w.c("s_pe")
                kvpe = [0, 0, 0, 0]
                for wi, (wh_p, wl_p, b0) in enumerate(
                        ((pbfA, pbfB, 0), (pbfC, pbfD, 2))):
                    for ch in range(2):
                        n0 = ch * 512
                        ops = []
                        for hc in range(HC):
                            yh = y2T(0, hc, 0)
                            yl = y2T(1, hc, 0)
                            wh = wh_p[:, hc * NBIN + n0: hc * NBIN + n0 + 512]
                            wl_ = wl_p[:, hc * NBIN + n0: hc * NBIN + n0 + 512]
                            ops += [(yh, wh), (yh, wl_), (yl, wh)]
                        ops += [(oner[:, :], bias6[:, wi * NBIN + n0:
                                                    wi * NBIN + n0 + 512]),
                                (oner[:, :], bias6[:, (3 + wi) * NBIN + n0:
                                                   (3 + wi) * NBIN + n0 + 512])]
                        kvpe[2 * wi + ch] = mm_group(
                            b0 + ch, PS[b0 + ch][:, :], ops,
                            extra_first_waits=(("s_ac", ac_T),))
                if t > 0:
                    b_kvsum(t - 1)
                if t + 1 < NT:
                    sp_dv[t + 1] = b_prep(t + 1)
                evac((0, 1), tf3[:, :], pk01[:, :], pe_count=kvpe[1],
                     extra_waits=(("s_dv", tmp_rel), ("s_gc", w.c("s_gc")),))
                ac_kv = evac((2, 3), tf4[:, :], pk23[:, :], pe_count=kvpe[3])
                kr, ki = tf3[:, 0:513], tf3[:, 513:1024]
                vr, vi = tf4[:, 0:513], tf4[:, 513:1024]
                kr1, vr1 = tf3[:, 1:512], tf4[:, 1:512]
                w.wait("vector", (("s_ac", ac_kv),))
                w.wait("gpsimd", (("s_ac", ac_kv), ("s_dv", b_gtmp_rel)))
                w.op("vector", "tensor_tensor", out=tf5[:, 0:513], in0=kr,
                     in1=vr, op=AX.mult, inc="s_dv")
                w.op("vector", "tensor_tensor", out=tf7[:, 0:511], in0=ki,
                     in1=vi, op=AX.mult, inc="s_dv")
                w.op("vector", "tensor_tensor", out=tf5[:, 1:512],
                     in0=tf5[:, 1:512], in1=tf7[:, 0:511], op=AX.subtract,
                     inc="s_dv")
                w.op("gpsimd", "tensor_tensor", out=tf5[:, 513:1024], in0=kr1,
                     in1=vi, op=AX.mult, inc="s_gc")
                w.op("gpsimd", "tensor_tensor", out=tf7[:, 513:1024], in0=ki,
                     in1=vr1, op=AX.mult, inc="s_gc")
                w.op("gpsimd", "tensor_tensor", out=tf5[:, 513:1024],
                     in0=tf5[:, 513:1024], in1=tf7[:, 513:1024], op=AX.add,
                     inc="s_gc")
                gc_im = w.c("s_gc")
                tmp_rel = w.c("s_dv")
                kvb_h = tb2 if t % 2 == 0 else tb4
                kvb_l = tb3 if t % 2 == 0 else tb5
                sksem = "s_sk0" if t % 2 == 0 else "s_sk1"
                dv_kv = split16(kvb_h[:, :], kvb_l[:, :], tf5[:, :],
                                extra_waits=((sksem, kv_st_rel[t % 2]),
                                             ("s_pe", kvb_rel[t % 2]),
                                             ("s_gc", gc_im)))
                b_gtmp_rel = dv_kv
                kv_dv[t % 2] = dv_kv
                w.op("gpsimd", "dma_start",
                     out=kvh_d[t * TILE:(t + 1) * TILE, :], in_=kvb_h[:, :],
                     waits=(("s_dv", dv_kv),), inc=sksem, incn=16)
                w.op("gpsimd", "dma_start",
                     out=kvl_d[t * TILE:(t + 1) * TILE, :], in_=kvb_l[:, :],
                     inc=sksem, incn=16)
                kv_st_rel[t % 2] = w.c(sksem)
            b_kvsum(NT - 1)
            kv_sum_pe = w.c("s_pe")

            ac_cs = evac(4, brow[0:1, 0:512], ps4[0:1, :], pe_count=kv_sum_pe)
            ac_cs = evac(5, brow[0:1, 512:1024], ps5[0:1, :])
            w.op("vector", "tensor_scalar", out=brow2[:, :], in0=brow[:, :],
                 scalar1=flags[0:1, 0:1], scalar2=None, op0=AX.mult,
                 op1=AX.bypass, waits=(("s_ac", ac_cs),), inc="s_dv")
            dv_cc = w.c("s_dv")
            w.op("gpsimd", "dma_start", out=cc_in[:, :], in_=brow2[:, :],
                 waits=(("s_dv", dv_cc),), inc="s_scc", incn=16)
            w.wait("gpsimd", (("s_scc", w.c("s_scc")),))
            groups = [[2 * i, 2 * i + 1] for i in range(n_cores // 2)]
            w.op("gpsimd", "collective_compute", "AllReduce", AX.add,
                 replica_groups=groups, ins=[cc_in.opt()], outs=[cc_out.opt()],
                 inc="s_cc")
            w.op("gpsimd", "dma_start", out=brow[:, :], in_=cc_out[:, :],
                 waits=(("s_cc", w.c("s_cc")),), inc="s_scc", incn=16)
            st_cc = ("s_scc", w.c("s_scc"))
            w.op("vector", "tensor_scalar", out=brow2[:, :], in0=brow[:, :],
                 scalar1=flags[0:1, 1:2], scalar2=None, op0=AX.mult,
                 op1=AX.bypass, waits=(st_cc,), inc="s_dv")
            cc_ok_dv = w.c("s_dv")

            for hc in range(HC):
                wl = (("s_pe", kv_sum_pe),) if hc == 0 else ()
                w.op("sync", "dma_start", out=pbfA[:, hc * NBIN:(hc + 1) * NBIN],
                     in_=awh_d[2][hc, :, :], waits=wl, inc="s_aw", incn=16)
                w.op("sync", "dma_start", out=pbfB[:, hc * NBIN:(hc + 1) * NBIN],
                     in_=awl_d[2][hc, :, :], inc="s_aw", incn=16)
                w.op("sync", "dma_start", out=pbfC[:, hc * NBIN:(hc + 1) * NBIN],
                     in_=awh_d[3][hc, :, :], inc="s_aw", incn=16)
                w.op("sync", "dma_start", out=pbfD[:, hc * NBIN:(hc + 1) * NBIN],
                     in_=awl_d[3][hc, :, :], inc="s_aw", incn=16)
            ld_aw2 = ("s_aw", w.c("s_aw"))

            c_r_rel = [w.c("s_dv"), w.c("s_dv"), w.c("s_dv")]
            c_kv_rel = [kv_sum_pe, kv_sum_pe]
            c_cum_ac = 0
            c_cum_dv = [0, 0]
            c_ai_st = 0
            c_y3_st = 0
            c_qc_ac = [0, 0]
            c_sp_dv = {}
            c_qpe = {}
            prev_c_split = w.c("s_dv")
            dv_cb = split16(carryh[:, :], carryl[:, :], brow2[:, :],
                            extra_waits=(("s_dv", cc_ok_dv),))
            for ch in range(2):
                n0 = ch * 512
                for i, part in enumerate((carryh, carryl)):
                    w.op("tensor", "matmul", out=PS[4 + ch][0:1, :],
                         lhsT=onec[0:1, 0:1], rhs=part[0:1, n0:n0 + 512],
                         start=(i == 0), stop=False, skip_group_check=True,
                         waits=(brel[4 + ch], ("s_dv", dv_cb)) if i == 0 else (),
                         inc="s_pe" if i == 1 else None)
            carry_pe = w.c("s_pe")
            carry_split_pe = 0

            def c_rb(t):
                m = t % 3
                if m == 0:
                    return tf0[:, :]
                if m == 1:
                    return tf1[:, :]
                return pf32[:, 0:D]

            def c_prep_load(t):
                kvb_h = tb2 if t % 2 == 0 else tb4
                kvb_l = tb3 if t % 2 == 0 else tb5
                xsem = "s_x0" if t % 2 == 0 else "s_x1"
                ksem = "s_k0" if t % 2 == 0 else "s_k1"
                sksem = "s_sk0" if t % 2 == 0 else "s_sk1"
                w.op("sync", "dma_start", out=c_rb(t),
                     in_=r1_d[t * TILE:(t + 1) * TILE, :],
                     waits=(ld_aw2, ("s_dv", c_r_rel[t % 3]),
                            ("s_pe", kv_sum_pe),
                            ("s_ac", w.c("s_ac") if t < 2 else 0)),
                     inc=xsem, incn=16)
                w.op("sync", "dma_start", out=kvb_h[:, :],
                     in_=kvh_d[t * TILE:(t + 1) * TILE, :],
                     waits=(("s_pe", c_kv_rel[t % 2]),
                            (sksem, w.c(sksem) if t < 2 else 0)),
                     inc=ksem, incn=16)
                w.op("sync", "dma_start", out=kvb_l[:, :],
                     in_=kvl_d[t * TILE:(t + 1) * TILE, :],
                     inc=ksem, incn=16)
                return ((xsem, w.c(xsem)), (ksem, w.c(ksem)))

            def c_prep_y2(t, ld_r):
                nonlocal prev_c_split
                sc_col = ln2sb[:, 2 * t:2 * t + 1]
                bi_col = ln2sb[:, 2 * t + 1:2 * t + 2]
                w.op("scalar", "activation", out=tf2[:, :], in_=c_rb(t),
                     func=AF.Identity, bias=bi_col, scale=sc_col,
                     waits=(ld_r, ("s_dv", prev_c_split)), inc="s_ac")
                ac_y2 = w.c("s_ac")
                w.wait("vector", (("s_ac", ac_y2),))
                dv_sp = split16(tb0[:, :], tb1[:, :], tf2[:, :],
                                extra_waits=(("s_pe", tr_pe_rel),))
                prev_c_split = dv_sp
                return dv_sp

            c_acT = {}

            def c_y2T(t):
                nonlocal tr_pe_rel
                c_acT[t] = transpose8(y2T_full, tb0, tb1, 0,
                                      first_waits=(("s_dv", c_sp_dv[t]),))
                tr_pe_rel = w.c("s_pe")

            def c_q_mm(t):
                ac_T = c_acT[t]
                qpe = [0, 0]
                for ch in range(2):
                    n0 = ch * 512
                    ops = []
                    for hc in range(HC):
                        yh = y2T(0, hc, 0)
                        yl = y2T(1, hc, 0)
                        wh = pbfA[:, hc * NBIN + n0: hc * NBIN + n0 + 512]
                        wl_ = pbfB[:, hc * NBIN + n0: hc * NBIN + n0 + 512]
                        ops += [(yh, wh), (yh, wl_), (yl, wh)]
                    ops += [(oner[:, :], bias6[:, 2 * NBIN + n0:
                                                2 * NBIN + n0 + 512]),
                            (oner[:, :], bias6[:, 5 * NBIN + n0:
                                               5 * NBIN + n0 + 512])]
                    qpe[ch] = mm_group(ch, PS[ch][:, :], ops,
                                       extra_first_waits=(("s_ac", ac_T),))
                c_qpe[t] = qpe

            def c_qconj(t):
                qc = tf7 if t % 2 == 0 else tf9
                qpe = c_qpe[t]
                evac((0, 1), qc[:, 0:513], pk01[:, 0:513], pe_count=qpe[1],
                     extra_waits=(("s_dv", c_cum_dv[t % 2]),))
                w.op("scalar", "activation", out=qc[:, 513:1024],
                     in_=pk01[:, 513:1024], func=AF.Identity, bias=0.0,
                     scale=-1.0, inc="s_ac")
                brel[0] = ("s_ac", w.c("s_ac"))
                brel[1] = ("s_ac", w.c("s_ac"))
                c_qc_ac[t % 2] = w.c("s_ac")

            c_cum_ac_arr = [0, 0]
            dv_carry_state = 0
            c_mrel = {}

            def c_carry_prep():
                nonlocal dv_carry_state
                w.op("scalar", "copy", out=brow[0:1, 0:512], in_=ps4[0:1, :],
                     waits=(("s_pe", carry_pe),), inc="s_ac")
                w.op("scalar", "copy", out=brow[0:1, 512:1024], in_=ps5[0:1, :],
                     inc="s_ac")
                ac_carry = w.c("s_ac")
                dv_carry_state = split16(
                    carryh[:, :], carryl[:, :], brow[:, :],
                    extra_waits=(("s_ac", ac_carry),
                                 ("s_pe", carry_split_pe)))

            c_cpe = {}

            def c_cum_mm(t):
                nonlocal carry_pe, carry_split_pe
                kvb_h = tb2 if t % 2 == 0 else tb4
                kvb_l = tb3 if t % 2 == 0 else tb5
                ld_c = c_ld[t][1]
                cpe = [0, 0]
                for ch in range(2):
                    n0 = ch * 512
                    ops = [(oner[:, :], carryh[0:1, n0:n0 + 512]),
                           (oner[:, :], carryl[0:1, n0:n0 + 512]),
                           (ltri, kvb_h[:, n0:n0 + 512]),
                           (ltri, kvb_l[:, n0:n0 + 512])]
                    cpe[ch] = mm_group(2 + ch, PS[2 + ch][:, :], ops,
                                       extra_first_waits=(("s_dv", dv_carry_state),
                                                          ld_c))
                carry_split_pe = w.c("s_pe")
                for ch in range(2):
                    n0 = ch * 512
                    for part in (kvb_h, kvb_l):
                        w.op("tensor", "matmul", out=PS[4 + ch][0:1, :],
                             lhsT=onec[:, 0:1], rhs=part[:, n0:n0 + 512],
                             start=False, stop=(t == NT - 1),
                             skip_group_check=True,
                             inc="s_pe" if (ch == 1 and part is kvb_l) else None)
                carry_pe = w.c("s_pe")
                c_kv_rel[t % 2] = carry_pe
                c_cpe[t] = cpe

            def c_cum_fin(t):
                cumb2 = tf5 if t % 2 == 0 else tf6
                c_cum_ac_arr[t % 2] = evac(
                    (2, 3), cumb2[:, :], pk23[:, :], pe_count=c_cpe[t][1],
                    extra_waits=(("s_dv", c_cum_dv[t % 2]),))
                if t + 1 < NT:
                    c_carry_prep()

            c_carry_prep()
            ld0_pair = c_prep_load(0)
            c_ld = {0: ld0_pair}
            c_sp_dv[0] = c_prep_y2(0, ld0_pair[0])
            c_y2T(0)
            c_q_mm(0)
            c_qconj(0)
            c_ld[1] = c_prep_load(1)
            c_sp_dv[1] = c_prep_y2(1, c_ld[1][0])
            c_y2T(1)
            c_q_mm(1)
            c_cum_mm(0)
            c_cum_fin(0)
            if NT > 1:
                c_cum_mm(1)

            for t in range(NT):
                qc = tf7 if t % 2 == 0 else tf9
                cumb = tf5 if t % 2 == 0 else tf6
                if t + 2 < NT:
                    c_ld[t + 2] = c_prep_load(t + 2)
                ar, ai = cumb[:, 0:513], cumb[:, 513:1024]
                br_, bi_ = qc[:, 0:513], qc[:, 513:1024]
                ar1, br1 = cumb[:, 1:512], qc[:, 1:512]
                w.wait("vector", (("s_ac", c_cum_ac_arr[t % 2]),
                                  ("s_sa", c_ai_st),
                                  ("s_ac", c_qc_ac[t % 2])))
                w.wait("gpsimd", (("s_ac", c_cum_ac_arr[t % 2]),
                                  ("s_ac", c_qc_ac[t % 2]),
                                  ("s_sa", c_ai_st),
                                  ("s_dv", c_cum_dv[(t + 1) % 2])))
                w.op("vector", "tensor_tensor", out=tf8[:, 0:513], in0=ar,
                     in1=br_, op=AX.mult, inc="s_dv")
                w.op("vector", "tensor_tensor", out=tf3[:, 0:511], in0=ai,
                     in1=bi_, op=AX.mult, inc="s_dv")
                w.op("vector", "tensor_tensor", out=tf8[:, 1:512],
                     in0=tf8[:, 1:512], in1=tf3[:, 0:511], op=AX.subtract,
                     inc="s_dv")
                w.op("gpsimd", "tensor_tensor", out=tf8[:, 513:1024], in0=ai,
                     in1=br1, op=AX.mult, inc="s_gc")
                w.op("gpsimd", "tensor_tensor", out=tf3[:, 513:1024], in0=ar1,
                     in1=bi_, op=AX.mult, inc="s_gc")
                w.op("gpsimd", "tensor_tensor", out=tf8[:, 513:1024],
                     in0=tf8[:, 513:1024], in1=tf3[:, 513:1024], op=AX.add,
                     inc="s_gc")
                c_gc_im = w.c("s_gc")
                dv_qq = w.c("s_dv")
                dv_qs = split16(tb0[:, :], tb1[:, :], tf8[:, :],
                                extra_waits=(("s_pe", tr_pe_rel),
                                             ("s_gc", c_gc_im)))
                ac_T2 = transpose8(y3T_full, tb0, tb1, 0,
                                   first_waits=(("s_dv", dv_qs),))
                tr_pe_rel = w.c("s_pe")
                if t + 1 < NT:
                    c_qconj(t + 1)
                if t + 2 < NT:
                    c_sp_dv[t + 2] = c_prep_y2(t + 2, c_ld[t + 2][0])
                w.op("scalar", "activation", out=tf3[:, :], in_=tf8[:, :],
                     func=AF.Square, bias=0.0, scale=1.0,
                     waits=(("s_dv", dv_qq), ("s_gc", c_gc_im)), inc="s_ac")
                ac_sq = w.c("s_ac")
                w.op("vector", "reduce_sum", out=stat[:, 18:19], in_=tf3[:, :],
                     axis=AXL.X, waits=(("s_ac", ac_sq),), inc="s_dv")
                w.op("vector", "tensor_scalar", out=stat[:, 19:20],
                     in0=stat[:, 18:19], scalar1=2.0, scalar2=None,
                     op0=AX.mult, op1=AX.bypass, inc="s_dv")
                w.op("vector", "tensor_tensor", out=stat[:, 19:20],
                     in0=stat[:, 19:20], in1=tf3[:, 0:1], op=AX.subtract,
                     inc="s_dv")
                w.op("vector", "tensor_tensor", out=stat[:, 19:20],
                     in0=stat[:, 19:20], in1=tf3[:, 512:513], op=AX.subtract,
                     inc="s_dv")
                dv_nrm = w.c("s_dv")
                w.op("scalar", "activation", out=stat[:, 20:21],
                     in_=stat[:, 19:20], func=AF.Sqrt, bias=0.0, scale=1.0,
                     waits=(("s_dv", dv_nrm),), inc="s_ac")
                ac_nr = w.c("s_ac")
                w.op("vector", "tensor_scalar", out=stat[:, 21:22],
                     in0=stat[:, 20:21], scalar1=1e-9, scalar2=None,
                     op0=AX.add, op1=AX.bypass, waits=(("s_ac", ac_nr),),
                     inc="s_dv")
                w.op("vector", "reciprocal", out=stat[:, 21:22],
                     in_=stat[:, 21:22], inc="s_dv")
                dv_ri = w.c("s_dv")
                ipe = [0, 0]
                for ch in range(2):
                    n0 = ch * 512
                    ops = []
                    for hc in range(HC):
                        yh = y3Tsb(0, hc, 0)
                        yl = y3Tsb(1, hc, 0)
                        wh = pbfC[:, hc * NBIN + n0: hc * NBIN + n0 + 512]
                        wl_ = pbfD[:, hc * NBIN + n0: hc * NBIN + n0 + 512]
                        ops += [(yh, wh), (yh, wl_), (yl, wh)]
                    ipe[ch] = mm_group(ch, PS[ch][:, :], ops,
                                       extra_first_waits=(("s_ac", ac_T2),
                                                          ("s_ac", max(c_qc_ac)),))
                if t + 2 < NT:
                    c_y2T(t + 2)
                w.op("vector", "scalar_tensor_tensor", out=cumb[:, :],
                     in0=pk01[:, :], scalar=stat[:, 21:22], op0=AX.mult,
                     op1=AX.add, in1=c_rb(t),
                     waits=(("s_pe", ipe[1]), ("s_dv", dv_ri)), inc="s_dv")
                c_r_rel[t % 3] = w.c("s_dv")
                brel[0] = ("s_dv", w.c("s_dv"))
                brel[1] = ("s_dv", w.c("s_dv"))
                if t + 1 < NT:
                    c_cum_fin(t + 1)
                if t + 2 < NT:
                    c_q_mm(t + 2)
                ln_stats(w, cumb, stat[:, 22:23], stat[:, 23:24])
                w.op("scalar", "activation", out=tf4[:, :], in_=cumb[:, :],
                     func=AF.Identity, bias=stat[:, 23:24],
                     scale=stat[:, 22:23],
                     waits=(("s_dv", w.c("s_dv")), ("s_sy", c_y3_st)),
                     inc="s_ac")
                ac_y3 = w.c("s_ac")
                w.op("gpsimd", "dma_start",
                     out=y3x_d[t * TILE:(t + 1) * TILE, :], in_=tf4[:, :],
                     waits=(("s_ac", ac_y3),), inc="s_sy", incn=16)
                c_y3_st = w.c("s_sy")
                w.op("vector", "scalar_tensor_tensor", out=tf8[:, :],
                     in0=tf4[:, :], scalar=0.5, in1=cumb[:, :], op0=AX.mult,
                     op1=AX.add, waits=(("s_ac", ac_y3), ("s_sa", c_ai_st)),
                     inc="s_dv")
                dv_ai = w.c("s_dv")
                c_cum_dv[t % 2] = dv_ai
                w.op("gpsimd", "dma_start",
                     out=out_e[t * TILE:(t + 1) * TILE, :], in_=tf8[:, :],
                     waits=(("s_dv", dv_ai),), inc="s_sa", incn=16)
                c_ai_st = w.c("s_sa")
                w.wait("vector", (("s_ac", ac_y3),))
                dv_y3 = split16(tb0[:, :], tb1[:, :], tf4[:, :],
                                extra_waits=(("s_pe", tr_pe_rel),))
                ac_y3T = transpose8(y3T_full, tb0, tb1, 0,
                                    first_waits=(("s_dv", dv_y3),))
                tr_pe_rel = w.c("s_pe")
                ops = []
                for hc in range(HC):
                    gh = gwt[:, hc * NE:(hc + 1) * NE]
                    gl = gwt[:, 64 + hc * NE: 64 + (hc + 1) * NE]
                    ops += [(gh, y3Tsb(0, hc, 0)), (gh, y3Tsb(1, hc, 0)),
                            (gl, y3Tsb(0, hc, 0))]
                pe_s3 = mm_group(2, ps2[0:NE, 0:TILE], ops,
                                 extra_first_waits=(("s_ac", ac_y3T),))
                ac_s3 = evac(2, scsb[:, 0:TILE], ps2[0:NE, 0:TILE],
                             pe_count=pe_s3)
                w.op("tensor", "transpose", out=ps3[:, 0:NE],
                     in_=scsb[0:NE, 0:TILE], identity=identf8,
                     waits=(brel[3], ("s_ac", ac_s3)), inc="s_pe")
                pe_s4 = w.c("s_pe")
                w.op("vector", "tensor_reduce", out=stat[:, 17:18],
                     in_=ps3[:, 0:NE], axis=AXL.X, op=AX.max,
                     waits=(("s_pe", pe_s4),), inc="s_dv")
                w.op("vector", "tensor_scalar",
                     out=mask3[:, t * NE:(t + 1) * NE], in0=ps3[:, 0:NE],
                     scalar1=stat[:, 17:18], scalar2=None, op0=AX.is_equal,
                     op1=AX.bypass, inc="s_dv")
                c_mrel[t] = w.c("s_dv")
                brel[2] = ("s_dv", w.c("s_dv"))
                brel[3] = ("s_dv", w.c("s_dv"))
                if t + 2 < NT:
                    c_cum_mm(t + 2)

            tab_ig2 = emit_positions(mask3, c_mrel, w.c("s_pe"))
            ffn_gather(out_e, y3x_d, tab_ig2,
                       src_waits=(("s_sy", w.c("s_sy")),),
                       dst_waits=(("s_sa", w.c("s_sa")),))

            w.wait("gpsimd", (("s_sr", w.c("s_sr")), ("s_sk0", w.c("s_sk0")),
                              ("s_sk1", w.c("s_sk1")), ("s_sa", w.c("s_sa")),
                              ("s_sy", w.c("s_sy")), ("s_sp", w.c("s_sp")),
                              ("s_scc", w.c("s_scc")), ("s_ig", w.c("s_ig")),
                              ("s_ig2", w.c("s_ig2")),
                              ("s_g0", w.c("s_g0")), ("s_g1", w.c("s_g1"))))

        def run_pass(name, eng):
            w = W(name, {name: eng}, sems)
            emit_all(w)

        @block.sync
        def _(sync):
            run_pass("sync", sync)

        @block.tensor
        def _(tensor):
            run_pass("tensor", tensor)

        @block.scalar
        def _(scalar):
            run_pass("scalar", scalar)

        @block.vector
        def _(vector):
            run_pass("vector", vector)

        @block.gpsimd
        def _(gpsimd):
            run_pass("gpsimd", gpsimd)

    return nc


def kernel(L0_attn_x, L0_attn_Wk, L0_attn_bk, L0_attn_Wv, L0_attn_bv,
           L0_attn_Wq, L0_attn_bq, L0_ffn_gate_w, L0_ffn_expert_w):
    CS, MINV, EYEL = _host_constants()
    x = np.ascontiguousarray(np.asarray(L0_attn_x, dtype=np.float32))
    B, S, _ = x.shape
    n_cores = 8
    half = S // 2
    nc = build_nc(n_cores=n_cores, n_tokens=half, G=1024)
    CAPh = max(128, (3 * half) // 16)
    ntil = half // 128
    tok = np.empty((128, ntil), np.uint32)
    for c in range(ntil):
        tok[:, c] = c * 128 + np.arange(128)
    eCb_a = np.empty((128, 8), np.float32)
    for e in range(8):
        eCb_a[:, e] = e * CAPh
    tabf = np.full((8 * CAPh + 1, 1), half, np.uint32)
    base = {
        "MOEI": tok, "MOEF": eCb_a, "TABF": tabf,
        "Wk": np.ascontiguousarray(np.asarray(L0_attn_Wk, dtype=np.float32)),
        "Wv": np.ascontiguousarray(np.asarray(L0_attn_Wv, dtype=np.float32)),
        "Wq": np.ascontiguousarray(np.asarray(L0_attn_Wq, dtype=np.float32)),
        "bk": np.ascontiguousarray(np.asarray(L0_attn_bk, dtype=np.float32)).reshape(1, D),
        "bv": np.ascontiguousarray(np.asarray(L0_attn_bv, dtype=np.float32)).reshape(1, D),
        "bq": np.ascontiguousarray(np.asarray(L0_attn_bq, dtype=np.float32)).reshape(1, D),
        "gate_w": np.ascontiguousarray(np.asarray(L0_ffn_gate_w, dtype=np.float32)),
        "expert_w": np.ascontiguousarray(np.asarray(L0_ffn_expert_w, dtype=np.float32)),
        "CS": CS, "MINV": MINV, "EYEL": EYEL,
    }
    in_maps = []
    for c in range(n_cores):
        b, h = c // 2, c % 2
        m = dict(base)
        m["x"] = np.ascontiguousarray(x[b, h * half:(h + 1) * half, :])
        m["ccflags"] = np.array([[1.0 - h, float(h)]], dtype=np.float32)
        in_maps.append(m)
    res = run_bass_kernel_spmd(nc, in_maps, core_ids=list(range(n_cores)))
    out = np.empty((B, S, D), dtype=np.float32)
    for c in range(n_cores):
        b, h = c // 2, c % 2
        out[b, h * half:(h + 1) * half, :] = res.results[c]["out"]
    return out
